# version 07
# baseline (speedup 1.0000x reference)
import math
import contextlib
import numpy as np
import ml_dtypes
import concourse.bass as bass
import concourse.mybir as mybir
from concourse.bass_utils import run_bass_kernel_spmd

F32 = mybir.dt.float32
BF16 = mybir.dt.bfloat16
AF = mybir.ActivationFunctionType
ALU = mybir.AluOpType
AX = mybir.AxisListType

S = 2048
E = 2048
NH = 16
D = 128
HID = 5120
ET = E // 128
HT = HID // 128
NEG_BIAS = -8.0
MASKNEG = -1.0e9
INV_SQRT_D = 1.0 / math.sqrt(D)
N_FC = float(HID * E)
N_WP = float(E * HID)
HALF = 1024

ARENA_F32_COLS = 47616


class Prog:
    ENGS = ("pe", "act", "dve", "sync", "pool")

    def __init__(self):
        self.ops = {e: [] for e in self.ENGS}
        self.cnt = {}

    def emit(self, eng, fn, waits=(), inc=None, amt=1):
        wd = {}
        for (s, v) in waits:
            if v is not None and v > 0:
                wd[s] = max(wd.get(s, 0), int(v))
        ws = list(wd.items())
        self.ops[eng].append((ws, fn, inc, amt))
        if inc is not None:
            self.cnt[inc] = self.cnt.get(inc, 0) + amt
            return self.cnt[inc]
        return None

    def c(self, sem):
        return self.cnt.get(sem, 0)


def build_nc(n_heads=NH, stage2=True, dbg=None):
    nc = bass.Bass("TRN2", target_bir_lowering=False)

    XT = nc.dram_tensor("xt_in", [E, S], F32, kind="ExternalInput").ap()
    WQKV = nc.dram_tensor("wqkv_h", [NH, E, 3 * D], F32, kind="ExternalInput").ap()
    WOCB = nc.dram_tensor("wo_cb", [ET, E, 128], F32, kind="ExternalInput").ap()
    WFCCB = nc.dram_tensor("wfc_cb", [HT, E, 128], F32, kind="ExternalInput").ap()
    WPJCB = nc.dram_tensor("wpj_cb", [ET, HID, 128], F32, kind="ExternalInput").ap()
    TCC = nc.dram_tensor("rope_cc", [128, S], F32, kind="ExternalInput").ap()
    TSS = nc.dram_tensor("rope_ss", [128, S], F32, kind="ExternalInput").ap()
    MASKIN = nc.dram_tensor("mask_in", [128, 128], F32, kind="ExternalInput").ap()
    IDIN = nc.dram_tensor("ident_in", [128, 128], F32, kind="ExternalInput").ap()
    YT = nc.dram_tensor("yt_out", [E, S], F32, kind="ExternalOutput").ap()
    OSCR = nc.dram_tensor("o_scr", [NH, 128, S], BF16, kind="Internal").ap()
    LNSCR = nc.dram_tensor("ln1_scr", [E, S], F32, kind="Internal").ap()
    ODBG = None
    if dbg == "oscr":
        ODBG = nc.dram_tensor("o_dbg", [NH, 128, S], F32, kind="ExternalOutput").ap()
    ZDBG = MDBG = LDBG = O2DBG = Z3DBG = HDBG = ZBDBG = WQDBG = None
    if dbg == "s2":
        ZDBG = nc.dram_tensor("z_dbg", [E, S], F32, kind="ExternalOutput").ap()
        MDBG = nc.dram_tensor("m_dbg", [128, 16], F32, kind="ExternalOutput").ap()
        LDBG = nc.dram_tensor("l_dbg", [E, S], F32, kind="ExternalOutput").ap()
        O2DBG = nc.dram_tensor("o2_dbg", [128, 16384], F32, kind="ExternalOutput").ap()
        Z3DBG = nc.dram_tensor("z3_dbg", [E, S], F32, kind="ExternalOutput").ap()
        HDBG = nc.dram_tensor("h_dbg", [128, 4096], F32, kind="ExternalOutput").ap()
        ZBDBG = nc.dram_tensor("zb_dbg", [128, 16384], F32, kind="ExternalOutput").ap()
        WQDBG = nc.dram_tensor("wq_dbg", [128, 2048], F32, kind="ExternalOutput").ap()

    p = Prog()
    ctx = contextlib.ExitStack()

    RAW = ctx.enter_context(nc.sbuf_tensor("arena", [128, ARENA_F32_COLS], F32))
    RAWB = RAW[:].bitcast(BF16)

    def f32(off, cols):
        return RAW[:, off:off + cols]

    def bf(off, cols):
        return RAWB[:, 2 * off:2 * off + cols]

    cc_sb = ctx.enter_context(nc.sbuf_tensor("cc_sb", [128, S], BF16))
    ss_sb = ctx.enter_context(nc.sbuf_tensor("ss_sb", [128, S], BF16))
    mask_sb = ctx.enter_context(nc.sbuf_tensor("mask_sb", [128, 128], F32))
    id_sb = ctx.enter_context(nc.sbuf_tensor("id_sb", [128, 128], BF16))
    ones_bcol = ctx.enter_context(nc.sbuf_tensor("ones_bcol", [128, 1], BF16))
    ones_fcol = ctx.enter_context(nc.sbuf_tensor("ones_fcol", [128, 1], F32))
    ones_frow = ctx.enter_context(nc.sbuf_tensor("ones_frow", [1, 128], F32))
    acc_fc = ctx.enter_context(nc.sbuf_tensor("acc_fc", [128, 80], F32))
    acc_wp = ctx.enter_context(nc.sbuf_tensor("acc_wp", [128, 80], F32))
    accT = ctx.enter_context(nc.sbuf_tensor("accT", [128, 2], F32))
    rows = ctx.enter_context(nc.sbuf_tensor("rows_sm", [1, 8], F32))
    thr_fc = ctx.enter_context(nc.sbuf_tensor("thr_fc", [128, 2], F32))
    thr_wp = ctx.enter_context(nc.sbuf_tensor("thr_wp", [128, 2], F32))
    c_vec = ctx.enter_context(nc.sbuf_tensor("c_vec", [128, 1], F32))
    rl_sb = ctx.enter_context(nc.sbuf_tensor("rl_sb", [128, 2], F32))
    consts = ctx.enter_context(nc.sbuf_tensor("consts", [128, 4], F32))
    lnrows = ctx.enter_context(nc.sbuf_tensor("lnrows", [1, 2560], F32))
    LR = {"m": lnrows[0:1, 0:512], "v": lnrows[0:1, 512:1024],
          "s": lnrows[0:1, 1024:1536], "A": lnrows[0:1, 1536:2048],
          "B": lnrows[0:1, 2048:2560]}

    PS = [ctx.enter_context(nc.psum_tensor(f"ps{i}", [128, 512], F32)) for i in range(8)]
    PSB = [PS[i][:].bitcast(BF16) for i in range(8)]

    o_Xb = 0
    o_WH0, o_WH1 = 16384, 19456
    o_Qd0, o_Qd1 = 22528, 23552
    o_Kd0, o_Kd1 = 24576, 25600
    o_Vd = 26624
    o_Vt0, o_Vt1 = 27648, 28736
    o_PT0, o_PT1 = 29824, 33920
    o_Ot = 38016
    o_Ofm = 38528
    o_Ff = 39552
    o_Fs = 41600
    o_tT = 43648
    o_tU = 44160
    assert 44672 <= ARENA_F32_COLS
    Xb = bf(o_Xb, 32768)
    WH = [bf(o_WH0, 6144), bf(o_WH1, 6144)]
    Qd = [bf(o_Qd0, 2048), bf(o_Qd1, 2048)]
    Kd = [bf(o_Kd0, 2048), bf(o_Kd1, 2048)]
    Vd = bf(o_Vd, 2048)
    Vt = [bf(o_Vt0, 2176), bf(o_Vt1, 2176)]
    PT = [bf(o_PT0, 8192), bf(o_PT1, 8192)]
    Ot = bf(o_Ot, 1024)
    Ofm = bf(o_Ofm, 2048)
    Ff = f32(o_Ff, 2048)
    Fs = f32(o_Fs, 2048)
    tT = f32(o_tT, 512)
    tU = f32(o_tU, 512)
    MB = [bf(o_Ff, 4096), bf(o_Fs, 4096)]

    o_Z = 0
    o_H1 = 16384
    o_T0 = 36864
    o_T1 = 39936
    o_TL = 42496
    o_T3 = 44544
    o_T4 = 45568
    assert 45568 + 2048 <= ARENA_F32_COLS
    Z = RAW[:, o_Z:o_Z + 16384]
    ZB = RAWB[:, 2 * o_Z:2 * o_Z + 32768]
    H1 = bf(o_H1, 40960)
    O2 = bf(o_H1, 16384)
    WoCB = [bf(o_H1 + 8192, 2048), bf(o_H1 + 9216, 2048)]
    LNW_A = f32(o_H1 + 10240, 1024)
    LNW_B = f32(o_H1 + 11264, 1024)
    LN1IN = [f32(o_TL, 1024), f32(o_TL + 1024, 1024)]
    WPQ = bf(o_T0, 5120)
    WFCQ = [bf(o_T0, 2048), bf(o_T0 + 1024, 2048)]
    LTMP_FC = bf(o_T0 + 2048, 2048)
    WPF = [f32(o_T1, 1280), f32(o_T1 + 1280, 1280)]
    WFCF = [f32(o_T1, 1024), f32(o_T1 + 1024, 1024)]
    HREL = [f32(o_T3, 512), f32(o_T3 + 512, 512)]
    LTMP_PJ = bf(o_T3, 2048)
    ZBS = [bf(o_T4, 1024), bf(o_T4 + 512, 1024), bf(o_T4 + 1024, 1024),
           bf(o_T4 + 1536, 1024)]

    def zslot(i):
        return Z[:, i * 1024:(i + 1) * 1024]

    bank_free = {}

    def bank_wait(i):
        w = bank_free.get(i)
        return [w] if w else []

    p.emit("pool", lambda e: e.dma_start(out=cc_sb[:], in_=TCC[:]), inc="sw", amt=16)
    p.emit("pool", lambda e: e.dma_start(out=ss_sb[:], in_=TSS[:]), inc="sw", amt=16)
    v_tabs = p.emit("pool", lambda e: e.dma_start(out=id_sb[:], in_=IDIN[:]),
                    inc="sw", amt=16)
    v_mask = p.emit("sync", lambda e: e.dma_start(out=mask_sb[:], in_=MASKIN[:]),
                    inc="hw", amt=16)

    XTr = XT.rearrange("(et pp) s -> pp et s", pp=128)
    v_xb = None
    for c4 in range(4):
        v_xb = p.emit("pool", lambda e, c4=c4: e.dma_start(
            out=Xb[:].rearrange("pp (et s) -> pp et s", s=2048)[:, c4 * 4:(c4 + 1) * 4, :],
            in_=XTr[:, c4 * 4:(c4 + 1) * 4, :]), inc="sw", amt=16)

    def load_wh(h, waits):
        par = h % 2
        return p.emit("pool", lambda e, h=h, par=par: e.dma_start(
            out=WH[par][:].rearrange("pp (et c) -> pp et c", c=384),
            in_=WQKV[h].rearrange("(et pp) c -> pp et c", pp=128)),
            waits=waits, inc="sw", amt=16)

    v_wh = [None] * (NH + 1)
    v_wh[0] = load_wh(0, [])

    p.emit("dve", lambda e: e.memset(consts[:, 0:1], NEG_BIAS))
    p.emit("dve", lambda e: e.memset(consts[:, 1:2], INV_SQRT_D))
    p.emit("dve", lambda e: e.memset(consts[:, 2:3], -0.5))
    v_consts = p.emit("dve", lambda e: e.memset(consts[:, 3:4], 1e-5), inc="dve")
    p.emit("dve", lambda e: e.memset(ones_bcol[:, :], 1.0))
    p.emit("dve", lambda e: e.memset(ones_fcol[:, :], 1.0))
    p.emit("dve", lambda e: e.memset(ones_frow[:, :], 1.0))
    p.emit("dve", lambda e: e.memset(acc_fc[:, :], 0.0))
    p.emit("dve", lambda e: e.memset(acc_wp[:, :], 0.0))
    mzv = None
    for par in range(2):
        vt_v = Vt[par].rearrange("pp (kt c) -> pp kt c", c=136)
        mzv = p.emit("dve", lambda e, vt_v=vt_v: e.memset(vt_v[:, :, 128:136], 1.0),
                     inc="dve")

    st = {
        "qkv_last": [None] * NH,
        "rope_done": [None] * NH,
        "pv_prev": [None, None],
        "st_prev": [None, None],
        "strip_guard": [None, None],
        "ot_guard": None,
        "ofm_guard": None,
        "rl_guard": [None, None],
        "ff_guard": None,
        "mb_guard": [None, None],
        "mean_lv": [None, None],
        "mean_step": 0,
        "vtr_prev": None,
        "vt_ev": [[None, None], [None, None]],
        "mean_idx": 0,
        "mean_fc": None,
        "mean_wp": None,
        "did_fc_chain": False,
        "did_wp_chain": False,
        "trb": 6,
    }

    def next_trb():
        b = st["trb"]
        st["trb"] = 13 - b
        return b

    st["evac"] = {}
    st["rope_q"] = None
    A = {}

    def emit_qkv_tensor(h, t):
        par = h % 2
        toff = t * 128
        for sc in range(4):
            b = next_trb()
            waits = [("sw", v_xb), ("sw", v_wh[h])] + bank_wait(b)
            def mmq(e, par=par, toff=toff, sc=sc, b=b):
                r = None
                for et in range(ET):
                    r = e.matmul(
                        out=PS[b][:, 0:512],
                        lhsT=WH[par][:, et * 384 + toff:et * 384 + toff + 128],
                        rhs=Xb[:, et * 2048 + sc * 512:et * 2048 + (sc + 1) * 512],
                        start=(et == 0), stop=(et == ET - 1))
                return r
            v = p.emit("pe", mmq, waits=waits, inc="pe")
            if t < 2:
                w2 = [("pe", v)]
                if sc == 0 and t == 0 and st["ff_guard"] is not None:
                    w2.append(st["ff_guard"])
                if sc == 0 and t == 1:
                    w2.append(("dve", st["rope_q"]))
                ev = p.emit("act", lambda e, sc=sc, b=b: e.copy(
                    out=Ff[:, sc * 512:(sc + 1) * 512], in_=PS[b][:, 0:512]),
                    waits=w2, inc="act")
            else:
                w2 = [("pe", v)]
                if sc == 0:
                    w2.append(("pe", st["vtr_prev"]))
                ev = p.emit("act", lambda e, sc=sc, b=b: e.copy(
                    out=Vd[:, sc * 512:(sc + 1) * 512], in_=PS[b][:, 0:512]),
                    waits=w2, inc="act")
            st["evac"][(t, sc)] = ev
            bank_free[b] = ("act", ev)
        if t < 2:
            wsw = [("act", st["evac"][(t, 3)])]
            if t == 0 and st["ff_guard"] is not None:
                wsw.append(st["ff_guard"])
            else:
                wsw.append(("dve", st["rope_q"]))
            p.emit("sync", lambda e: e.dma_start(out=Fs[0:64, :], in_=Ff[64:128, :]),
                   waits=wsw, inc="hw", amt=16)
            swv = p.emit("sync", lambda e: e.dma_start(out=Fs[64:128, :], in_=Ff[0:64, :]),
                         inc="hw", amt=16)
            dstd = Qd[par] if t == 0 else Kd[par]
            rv = None
            for sc in range(4):
                w3 = []
                if sc == 0:
                    w3 = [("hw", swv), ("pe", st["st_prev"][par]), ("sw", v_tabs)]
                p.emit("dve", lambda e, sc=sc: e.tensor_tensor(
                    out=tT[:, :], in0=Ff[:, sc * 512:(sc + 1) * 512],
                    in1=cc_sb[:, sc * 512:(sc + 1) * 512], op=ALU.mult), waits=w3)
                p.emit("dve", lambda e, sc=sc: e.tensor_tensor(
                    out=tU[:, :], in0=Fs[:, sc * 512:(sc + 1) * 512],
                    in1=ss_sb[:, sc * 512:(sc + 1) * 512], op=ALU.mult))
                rv = p.emit("dve", lambda e, sc=sc, dstd=dstd: e.tensor_tensor(
                    out=dstd[:, sc * 512:(sc + 1) * 512], in0=tT[:, :], in1=tU[:, :],
                    op=ALU.add), inc="dve")
            if t == 0:
                st["rope_q"] = rv
            else:
                st["rope_done"][h] = rv

    def emit_vtrans(h):
        par = h % 2
        vlast = None
        for bt in range(2):
            b = next_trb()
            waits = [("act", st["evac"][(2, 2 * bt + 1)]), ("sw", v_tabs)] + bank_wait(b)
            def vtr(e, bt=bt, b=b):
                r = None
                for j in range(8):
                    si = bt * 8 + j
                    r = e.transpose(out=PSB[b][:, j * 128:(j + 1) * 128],
                                    in_=Vd[:, si * 128:(si + 1) * 128],
                                    identity=id_sb[:, :])
                return r
            v = p.emit("pe", vtr, waits=waits, inc="pe")
            vlast = v
            vt_v = Vt[par].rearrange("pp (kt c) -> pp kt c", c=136)
            w4 = [("pe", v), ("dve", mzv), ("pe", st["pv_prev"][par])]
            ev = p.emit("dve", lambda e, bt=bt, b=b, vt_v=vt_v: e.tensor_copy(
                out=vt_v[:, bt * 8:(bt + 1) * 8, 0:128], in_=PSB[b][:, 0:1024]),
                waits=w4, inc="dve")
            st["vt_ev"][par][bt] = ev
            bank_free[b] = ("dve", ev)
        st["vtr_prev"] = vlast
        st["qkv_last"][h] = p.c("pe")

    def mean_tick(h, n):
        for _ in range(n):
            step = st["mean_step"]
            ci = step // 5
            if ci >= 40:
                return
            j = step % 5
            st["mean_step"] += 1
            is_fc = ci < 20
            cc = ci if is_fc else ci - 20
            buf = MB[ci % 2]
            acc = acc_fc if is_fc else acc_wp
            if j == 0:
                srcf = (WFCCB if is_fc else WPJCB).rearrange("a b c -> (a b c)")
                srcv = srcf[ci % 2 * 0 + (cc) * 524288:(cc + 1) * 524288].rearrange(
                    "(pp f) -> pp f", pp=128)
                lw = [("dve", st["rope_done"][h])]
                if st["mb_guard"][ci % 2] is not None:
                    lw.append(st["mb_guard"][ci % 2])
                st["mean_lv"][ci % 2] = p.emit(
                    "pool", lambda e, buf=buf, srcv=srcv: e.dma_start(
                        out=buf[:, :], in_=srcv), waits=lw, inc="sw", amt=16)
            else:
                q4 = j - 1
                lv = st["mean_lv"][ci % 2]
                rv = p.emit("dve", lambda e, acc=acc, cc=cc, q4=q4, buf=buf:
                            e.tensor_reduce(
                    out=acc[:, cc * 4 + q4:cc * 4 + q4 + 1],
                    in_=buf[:, q4 * 1024:(q4 + 1) * 1024], axis=AX.X, op=ALU.add,
                    apply_absolute_value=True), waits=[("sw", lv)], inc="dve")
                if j == 4:
                    st["mb_guard"][ci % 2] = ("dve", rv)
                    st["ff_guard"] = ("dve", rv)
                    if ci == 19:
                        st["mean_fc"] = ("dve", rv)
                    if ci == 39:
                        st["mean_wp"] = ("dve", rv)

    def emit_thr_chain(which):
        acc = acc_fc if which == 0 else acc_wp
        nel = N_FC if which == 0 else N_WP
        done = st["mean_fc"] if which == 0 else st["mean_wp"]
        rv = p.emit("dve", lambda e: e.tensor_reduce(
            out=accT[:, which:which + 1], in_=acc[:, 0:80], axis=AX.X, op=ALU.add),
            waits=[done], inc="dve")
        b = 7
        v = p.emit("pe", lambda e: e.matmul(
            out=PS[b][0:1, 0:1], lhsT=ones_fcol[:, :], rhs=accT[:, which:which + 1],
            start=True, stop=True), waits=[("dve", rv)] + bank_wait(b), inc="pe")
        av = p.emit("dve", lambda e: e.tensor_scalar(
            out=rows[0:1, which:which + 1], in0=PS[b][0:1, 0:1],
            scalar1=1.0 / nel, scalar2=1e-5, op0=ALU.mult, op1=ALU.max),
            waits=[("pe", v)], inc="dve")
        bank_free[b] = ("dve", av)
        tv = p.emit("dve", lambda e: e.tensor_single_scalar(
            out=rows[0:1, 2 + which:3 + which], in_=rows[0:1, which:which + 1],
            scalar=0.7, op=ALU.mult), waits=[("dve", av)], inc="dve")
        tvec = thr_fc if which == 0 else thr_wp
        v2 = p.emit("pe", lambda e: e.matmul(
            out=PS[b][0:128, 0:1], lhsT=ones_frow[:, :],
            rhs=rows[0:1, 2 + which:3 + which], start=True, stop=True),
            waits=[("dve", tv)] + bank_wait(b), inc="pe")
        c1 = p.emit("act", lambda e, tvec=tvec: e.copy(
            out=tvec[:, 0:1], in_=PS[b][0:128, 0:1]), waits=[("pe", v2)], inc="act")
        e2 = p.emit("dve", lambda e, tvec=tvec: e.tensor_single_scalar(
            out=tvec[:, 1:2], in_=PS[b][0:128, 0:1], scalar=-1.0, op=ALU.mult),
            waits=[("pe", v2), ("act", c1)], inc="dve")
        bank_free[b] = ("dve", e2)
        if which == 1:
            s1 = p.emit("dve", lambda e: e.tensor_tensor(
                out=rows[0:1, 4:5], in0=rows[0:1, 0:1], in1=rows[0:1, 0:1], op=ALU.mult),
                waits=[("dve", av)], inc="dve")
            s2 = p.emit("dve", lambda e: e.tensor_tensor(
                out=rows[0:1, 5:6], in0=rows[0:1, 4:5], in1=rows[0:1, 1:2], op=ALU.mult),
                waits=[("dve", s1)], inc="dve")
            v3 = p.emit("pe", lambda e: e.matmul(
                out=PS[b][0:128, 4:5], lhsT=ones_frow[:, :], rhs=rows[0:1, 5:6],
                start=True, stop=True), waits=[("dve", s2)] + bank_wait(b), inc="pe")
            e3 = p.emit("act", lambda e: e.copy(out=c_vec[:, 0:1], in_=PS[b][0:128, 4:5]),
                        waits=[("pe", v3)], inc="act")
            bank_free[b] = ("act", e3)

    def attn_begin(hh):
        A.clear()
        A["stb"] = 0
        A["exp"] = {}
        A["pv"] = {}
        A["stlast"] = None

    def emit_attn_group(hh, g):
        par = hh % 2
        pg = g % 2
        exp_done = A["exp"]
        pv_qt = A["pv"]
        for kt in range(4 * g + 4):
            c0 = max(0, kt * 128 - g * 512)
            b = A["stb"]
            A["stb"] = (b + 1) % 4
            w = [("dve", st["rope_done"][hh])] + bank_wait(b)
            v = p.emit("pe", lambda e, par=par, g=g, kt=kt, c0=c0, b=b: e.matmul(
                out=PS[b][:, c0:512],
                lhsT=Kd[par][:, kt * 128:(kt + 1) * 128],
                rhs=Qd[par][:, g * 512 + c0:(g + 1) * 512],
                start=True, stop=True), waits=w, inc="pe")
            A["stlast"] = v
            wexp = [("pe", v), ("pe", st["strip_guard"][pg]), ("dve", v_consts)]
            if kt >= 4 * g:
                mv = p.emit("dve", lambda e, b=b, c0=c0: e.tensor_tensor(
                    out=PS[b][:, c0:c0 + 128], in0=PS[b][:, c0:c0 + 128],
                    in1=mask_sb[:, :], op=ALU.add),
                    waits=[("pe", v), ("hw", v_mask)], inc="dve")
                wexp = [("dve", mv), ("pe", st["strip_guard"][pg])]
            ev = p.emit("act", lambda e, pg=pg, kt=kt, c0=c0, b=b: e.activation(
                out=PT[pg][:, kt * 512 + c0:(kt + 1) * 512], in_=PS[b][:, c0:512],
                func=AF.Exp, bias=consts[:, 0:1], scale=consts[:, 1:2]),
                waits=wexp, inc="act")
            exp_done[(g, kt)] = ev
            bank_free[b] = ("act", ev)
        for qt in range(4 * g, 4 * g + 4):
            ob = 4 + qt % 2
            w = ([("act", exp_done[(g, qt)]), ("dve", st["vt_ev"][par][1])]
                 + bank_wait(ob))
            def pv(e, par=par, pg=pg, g=g, qt=qt, ob=ob):
                r = None
                for kt in range(qt + 1):
                    co = kt * 512 + qt * 128 - g * 512
                    r = e.matmul(out=PS[ob][:, 0:129],
                                 lhsT=PT[pg][:, co:co + 128],
                                 rhs=Vt[par][:, kt * 136:kt * 136 + 129],
                                 start=(kt == 0), stop=(kt == qt))
                return r
            v = p.emit("pe", pv, waits=w, inc="pe")
            pv_qt[qt] = v
            wr = [("pe", v), ("act", st["rl_guard"][qt % 2])]
            rv = p.emit("dve", lambda e, ob=ob, qt=qt: e.reciprocal(
                out=rl_sb[:, qt % 2:qt % 2 + 1], in_=PS[ob][:, 128:129]),
                waits=wr, inc="dve")
            wo = [("dve", rv)]
            if qt % 8 == 0:
                wo.append(("pe", st["ot_guard"]))
            ev = p.emit("act", lambda e, ob=ob, qt=qt: e.activation(
                out=Ot[:, (qt % 8) * 128:(qt % 8 + 1) * 128], in_=PS[ob][:, 0:128],
                func=AF.Copy, scale=rl_sb[:, qt % 2:qt % 2 + 1]),
                waits=wo, inc="act")
            st["rl_guard"][qt % 2] = ev
            bank_free[ob] = ("act", ev)
        st["strip_guard"][pg] = pv_qt[4 * g + 3]
        if g % 2 == 1:
            bb = g // 2
            b = next_trb()
            w = ([("act", st["rl_guard"][0]), ("act", st["rl_guard"][1])]
                 + bank_wait(b))
            def otr(e, b=b):
                r = None
                for j in range(8):
                    r = e.transpose(out=PSB[b][:, j * 128:(j + 1) * 128],
                                    in_=Ot[:, j * 128:(j + 1) * 128],
                                    identity=id_sb[:, :])
                return r
            v = p.emit("pe", otr, waits=w, inc="pe")
            st["ot_guard"] = v
            wf = [("pe", v)]
            if bb == 0:
                wf.append(("st", st["ofm_guard"]))
            ev = p.emit("dve", lambda e, bb=bb, b=b: e.tensor_copy(
                out=Ofm[:, bb * 1024:(bb + 1) * 1024], in_=PSB[b][:, 0:1024]),
                waits=wf, inc="dve")
            bank_free[b] = ("dve", ev)
            if bb == 1:
                st["ofm_guard"] = p.emit("sync", lambda e, hh=hh: e.dma_start(
                    out=OSCR[hh], in_=Ofm[:, :]), waits=[("dve", ev)],
                    inc="st", amt=16)
                if ODBG is not None:
                    p.emit("pool", lambda e, hh=hh: e.dma_start(
                        out=ODBG[hh], in_=Ofm[:, :]),
                        waits=[("dve", ev)], inc="st", amt=16)
        if g == 3:
            st["pv_prev"][par] = pv_qt[15]
            st["st_prev"][par] = A["stlast"]

    for h in range(n_heads + 1):
        a = h >= 1
        q = h < n_heads
        if a:
            attn_begin(h - 1)
            emit_attn_group(h - 1, 0)
        if q:
            if h + 1 < n_heads:
                wv = [("pe", st["qkv_last"][h - 1])] if h >= 1 else []
                v_wh[h + 1] = load_wh(h + 1, wv)
            emit_qkv_tensor(h, 0)
        if a:
            emit_attn_group(h - 1, 1)
        if q:
            emit_qkv_tensor(h, 1)
            if stage2:
                mean_tick(h, 5)
        if a:
            emit_attn_group(h - 1, 2)
        if q and stage2:
            mean_tick(h, 3)
        if q:
            emit_qkv_tensor(h, 2)
            emit_vtrans(h)
            if stage2:
                mean_tick(h, 4)
        if a:
            emit_attn_group(h - 1, 3)
        if q and stage2:
            mean_tick(h, 3)
            if st["mean_fc"] is not None and not st["did_fc_chain"]:
                emit_thr_chain(0)
                st["did_fc_chain"] = True
            if st["mean_wp"] is not None and not st["did_wp_chain"]:
                emit_thr_chain(1)
                st["did_wp_chain"] = True

    if stage2:
        BAR = [("pe", p.c("pe")), ("act", p.c("act")), ("dve", p.c("dve")),
               ("st", p.c("st")), ("hw", p.c("hw")), ("sw", p.c("sw"))]
        for eng in ("pe", "act", "dve", "sync", "pool"):
            own = {"pe": "pe", "act": "act", "dve": "dve", "sync": None,
                   "pool": None}[eng]
            w = [x for x in BAR if x[0] != own]
            p.emit(eng, lambda e: e.nop(), waits=w)

        s2 = {
            "z_free": [None] * ET,
            "z2b_cast": [None] * ET,
            "ln1_store": [None] * ET,
            "ln1_apply_last": None,
            "ln2_apply_last": None,
            "t0_guard": None,
            "t3_guard": None,
            "ln1in_guard": [None, None],
            "zbs_guard": [None, None],
            "wpf_guard": [None, None],
            "pq_done": [None] * 4,
            "wo_phase_end": None,
            "proj_end": None,
        }

        def emit_ln_chain(stats_stop, dst_A, dst_B):
            last = None
            for sc in range(2):
                sb_, qb_ = 2 + sc, 4 + sc
                p.emit("dve", lambda e, sb_=sb_: e.tensor_single_scalar(
                    out=LR["m"], in_=PS[sb_][0:1, 0:512], scalar=1.0 / E, op=ALU.mult),
                    waits=[("pe", stats_stop)])
                p.emit("dve", lambda e, qb_=qb_: e.tensor_single_scalar(
                    out=LR["v"], in_=PS[qb_][0:1, 0:512], scalar=1.0 / E, op=ALU.mult))
                p.emit("dve", lambda e: e.tensor_tensor(
                    out=LR["s"], in0=LR["m"], in1=LR["m"], op=ALU.mult))
                dv = p.emit("dve", lambda e: e.tensor_tensor(
                    out=LR["v"], in0=LR["v"], in1=LR["s"], op=ALU.subtract), inc="dve")
                av = p.emit("act", lambda e: e.activation(
                    out=LR["v"], in_=LR["v"], func=AF.Ln, bias=consts[0:1, 3:4]),
                    waits=[("dve", dv)], inc="act")
                av = p.emit("act", lambda e: e.activation(
                    out=LR["A"], in_=LR["v"], func=AF.Exp, scale=consts[0:1, 2:3]),
                    waits=[("act", av)], inc="act")
                bv = p.emit("dve", lambda e: e.scalar_tensor_tensor(
                    out=LR["B"], in0=LR["m"], scalar=-1.0, in1=LR["A"],
                    op0=ALU.mult, op1=ALU.mult), waits=[("act", av)], inc="dve")
                for (bk, src, dst) in ((6, LR["A"], dst_A), (7, LR["B"], dst_B)):
                    v1 = p.emit("pe", lambda e, bk=bk, src=src: e.matmul(
                        out=PS[bk][0:128, 0:512], lhsT=ones_frow[:, :], rhs=src,
                        start=True, stop=True), waits=[("dve", bv)] + bank_wait(bk),
                        inc="pe")
                    e1 = p.emit("act", lambda e, sc=sc, bk=bk, dst=dst: e.copy(
                        out=dst[:, sc * 512:(sc + 1) * 512], in_=PS[bk][0:128, 0:512]),
                        waits=[("pe", v1)], inc="act")
                    bank_free[bk] = ("act", e1)
                    last = e1
            for bk in (2, 3, 4, 5):
                bank_free[bk] = ("act", last)
            return last

        for half in range(2):
            hoff = half * HALF
            o2w = [("st", p.c("st"))]
            if half == 1:
                o2w.append(("pe", s2["proj_end"]))
            o2v = p.emit("sync", lambda e, hoff=hoff: e.dma_start(
                out=O2[:].rearrange("pp (et s) -> pp et s", s=1024),
                in_=OSCR[:, :, hoff:hoff + HALF].rearrange("et pp s -> pp et s")),
                waits=o2w, inc="hw", amt=16)
            if O2DBG is not None and half == 0:
                p.emit("pool", lambda e: e.dma_start(out=O2DBG[:], in_=O2[:, :]),
                       waits=[("hw", o2v)], inc="sw", amt=16)
            wo_mm_done = [None, None]
            stats_stop = None
            for ei in range(ET):
                parw = ei % 2
                wv = [("pe", wo_mm_done[parw])] if ei >= 2 else []
                if ei < 2 and half == 1:
                    wv.append(("pe", s2["proj_end"]))
                wcv = p.emit("pool", lambda e, ei=ei, parw=parw: e.dma_start(
                    out=WoCB[parw][:].rearrange("pp (et c) -> pp et c", c=128),
                    in_=WOCB[ei].rearrange("(et pp) c -> pp et c", pp=128)),
                    waits=wv, inc="sw", amt=16)
                xw = [("st", s2["z_free"][ei])] if s2["z_free"][ei] else []
                xv = p.emit("sync", lambda e, ei=ei, hoff=hoff: e.dma_start(
                    out=zslot(ei), in_=XT[ei * 128:(ei + 1) * 128, hoff:hoff + HALF]),
                    waits=xw, inc="hw", amt=16)
                zadd = None
                for sc in range(2):
                    b = sc
                    w = [("sw", wcv), ("hw", o2v)] + bank_wait(b)
                    def mmw(e, parw=parw, sc=sc, b=b):
                        r = None
                        for et in range(ET):
                            r = e.matmul(
                                out=PS[b][:, 0:512],
                                lhsT=WoCB[parw][:, et * 128:(et + 1) * 128],
                                rhs=O2[:, et * 1024 + sc * 512:et * 1024 + sc * 512 + 512],
                                start=(et == 0), stop=(et == ET - 1))
                        return r
                    v = p.emit("pe", mmw, waits=w, inc="pe")
                    zadd = p.emit("dve", lambda e, ei=ei, sc=sc, b=b: e.tensor_tensor(
                        out=zslot(ei)[:, sc * 512:(sc + 1) * 512],
                        in0=zslot(ei)[:, sc * 512:(sc + 1) * 512],
                        in1=PS[b][:, 0:512], op=ALU.add),
                        waits=[("pe", v), ("hw", xv)], inc="dve")
                    bank_free[b] = ("dve", zadd)
                wo_mm_done[parw] = p.c("pe")
                zpar = ei % 2
                zbw = [("dve", zadd), ("pe", s2["zbs_guard"][zpar])]
                p.emit("act", lambda e, ei=ei, zpar=zpar: e.copy(
                    out=ZBS[zpar], in_=zslot(ei)), waits=zbw, inc="act")
                sqv = p.emit("act", lambda e, ei=ei, zpar=zpar: e.square(
                    out=ZBS[2 + zpar], in_=zslot(ei)), inc="act")
                vlast = None
                for sc in range(2):
                    v = p.emit("pe", lambda e, sc=sc, ei=ei, zpar=zpar: e.matmul(
                        out=PS[2 + sc][0:1, 0:512], lhsT=ones_bcol[:, :],
                        rhs=ZBS[zpar][:, sc * 512:(sc + 1) * 512],
                        start=(ei == 0), stop=(ei == ET - 1), skip_group_check=True),
                        waits=[("act", sqv)], inc="pe")
                    v = p.emit("pe", lambda e, sc=sc, ei=ei, zpar=zpar: e.matmul(
                        out=PS[4 + sc][0:1, 0:512], lhsT=ones_bcol[:, :],
                        rhs=ZBS[2 + zpar][:, sc * 512:(sc + 1) * 512],
                        start=(ei == 0), stop=(ei == ET - 1), skip_group_check=True),
                        inc="pe")
                    vlast = v
                s2["zbs_guard"][zpar] = vlast
                if ZDBG is not None:
                    p.emit("pool", lambda e, ei=ei, hoff=hoff: e.dma_start(
                        out=ZDBG[ei * 128:(ei + 1) * 128, hoff:hoff + HALF],
                        in_=zslot(ei)), waits=[("act", sqv)], inc="sw", amt=16)
                if ei == ET - 1:
                    stats_stop = vlast
            s2["wo_phase_end"] = p.c("pe")
            lnd = emit_ln_chain(stats_stop, LNW_A, LNW_B)
            for ei in range(ET):
                p.emit("dve", lambda e, ei=ei: e.tensor_tensor(
                    out=zslot(ei), in0=zslot(ei), in1=LNW_A[:, :], op=ALU.mult),
                    waits=[("act", lnd)])
                av = p.emit("dve", lambda e, ei=ei: e.tensor_tensor(
                    out=zslot(ei), in0=zslot(ei), in1=LNW_B[:, :], op=ALU.add),
                    inc="dve")
                s2["ln1_apply_last"] = av
                sv = p.emit("sync", lambda e, ei=ei, hoff=hoff: e.dma_start(
                    out=LNSCR[ei * 128:(ei + 1) * 128, hoff:hoff + HALF],
                    in_=zslot(ei)), waits=[("dve", av)], inc="st", amt=16)
                s2["ln1_store"][ei] = sv
                cv = p.emit("act", lambda e, ei=ei: e.copy(
                    out=ZB[:, ei * 2048:ei * 2048 + 1024], in_=zslot(ei)),
                    waits=[("st", sv)], inc="act")
                s2["z2b_cast"][ei] = cv
            if ZBDBG is not None and half == 0:
                for et_ in range(ET):
                    p.emit("pool", lambda e, et_=et_: e.dma_start(
                        out=ZBDBG[:, et_ * 1024:(et_ + 1) * 1024],
                        in_=ZB[:, et_ * 2048:et_ * 2048 + 1024]),
                        waits=[("act", s2["z2b_cast"][ET - 1])], inc="sw", amt=16)
            fcq_free = [None, None]
            fc_mm_done = [None, None]
            last_tern = None
            h1_last = None
            h1_sq = [None, None]
            for hh in range(HT):
                parq = hh % 2
                lw = [("dve", last_tern)] if last_tern else []
                if hh == 0:
                    lw += [("dve", s2["wpf_guard"][0]), ("dve", s2["wpf_guard"][1])]
                lv1 = p.emit("sync", lambda e, hh=hh: e.dma_start(
                    out=WFCF[0][:, :].rearrange("pp (tt c) -> pp tt c", c=128),
                    in_=WFCCB[hh][0:1024, :].rearrange("(tt pp) c -> pp tt c", pp=128)),
                    waits=lw, inc="hw", amt=16)
                lv2 = p.emit("pool", lambda e, hh=hh: e.dma_start(
                    out=WFCF[1][:, :].rearrange("pp (tt c) -> pp tt c", c=128),
                    in_=WFCCB[hh][1024:2048, :].rearrange("(tt pp) c -> pp tt c", pp=128)),
                    waits=lw, inc="sw", amt=16)
                qv = None
                for hb in range(2):
                    tw = [("hw", lv1), ("sw", lv2)]
                    if hh == 0 and hb == 0:
                        tw.append(("pe", s2["t0_guard"]))
                    if hb == 0:
                        tw.append(("pe", fc_mm_done[parq]))
                    p.emit("dve", lambda e, hb=hb: e.tensor_single_scalar(
                        out=LTMP_FC[:, hb * 1024:(hb + 1) * 1024], in_=WFCF[hb][:, :],
                        scalar=thr_fc[:, 1:2], op=ALU.is_lt), waits=tw)
                    qv = p.emit("dve", lambda e, hb=hb, parq=parq: e.scalar_tensor_tensor(
                        out=WFCQ[parq][:, hb * 1024:(hb + 1) * 1024], in0=WFCF[hb][:, :],
                        scalar=thr_fc[:, 0:1], in1=LTMP_FC[:, hb * 1024:(hb + 1) * 1024],
                        op0=ALU.is_gt, op1=ALU.subtract), inc="dve")
                fcq_free[parq] = qv
                last_tern = qv
                if WQDBG is not None and half == 0 and hh == 0:
                    p.emit("pool", lambda e: e.dma_start(
                        out=WQDBG[:, :], in_=WFCQ[0][:, :]),
                        waits=[("dve", qv)], inc="sw", amt=16)
                for sc in range(2):
                    b = sc
                    w = ([("dve", qv), ("act", s2["z2b_cast"][ET - 1])]
                         + bank_wait(b))
                    def mmf(e, parq=parq, sc=sc, b=b):
                        r = None
                        for et in range(ET):
                            r = e.matmul(
                                out=PS[b][:, 0:512],
                                lhsT=WFCQ[parq][:, et * 128:(et + 1) * 128],
                                rhs=ZB[:, et * 2048 + sc * 512:et * 2048 + sc * 512 + 512],
                                start=(et == 0), stop=(et == ET - 1))
                        return r
                    v = p.emit("pe", mmf, waits=w, inc="pe")
                    fc_mm_done[parq] = v
                    rw = [("pe", v), ("act", h1_sq[sc])]
                    if hh == 0 and s2["t3_guard"] is not None:
                        rw.append(("act", s2["t3_guard"]))
                    rv = p.emit("dve", lambda e, sc=sc, b=b: e.tensor_scalar_max(
                        out=HREL[sc][:, :], in0=PS[b][:, 0:512], scalar1=0.0),
                        waits=rw, inc="dve")
                    bank_free[b] = ("dve", rv)
                    h1_last = p.emit("act", lambda e, hh=hh, sc=sc: e.square(
                        out=H1[:, hh * 1024 + sc * 512:hh * 1024 + sc * 512 + 512],
                        in_=HREL[sc][:, :]), waits=[("dve", rv)], inc="act")
                    h1_sq[sc] = h1_last
            s2["t0_guard"] = p.c("pe")
            if HDBG is not None and half == 0:
                p.emit("pool", lambda e: e.dma_start(out=HDBG[:, :], in_=H1[:, 0:4096]),
                       waits=[("act", h1_last)], inc="sw", amt=16)
            stats_stop2 = None
            wpf_guard = s2["wpf_guard"]
            pq_done = s2["pq_done"]
            for ei in range(ET):
                tern_q = [None] * 4
                for qb in range(4):
                    w = []
                    if ei == 0 and qb <= 1:
                        w.append(("dve", fcq_free[0]))
                        w.append(("dve", fcq_free[1]))
                    if wpf_guard[qb % 2] is not None:
                        w.append(("dve", wpf_guard[qb % 2]))
                    qu = "sync" if qb % 2 == 0 else "pool"
                    qs = "hw" if qb % 2 == 0 else "sw"
                    lv = p.emit(qu, lambda e, ei=ei, qb=qb: e.dma_start(
                        out=WPF[qb % 2][:, :].rearrange("pp (tt c) -> pp tt c", c=128),
                        in_=WPJCB[ei][qb * 1280:(qb + 1) * 1280, :].rearrange(
                            "(tt pp) c -> pp tt c", pp=128)),
                        waits=w, inc=qs, amt=16)
                    qv = None
                    for ch in range(2):
                        tw = [(qs, lv), ("pe", pq_done[qb])]
                        if ei == 0 and qb == 0 and ch == 0:
                            tw += [("pe", s2["t0_guard"]), ("act", h1_last)]
                        p.emit("dve", lambda e, qb=qb, ch=ch: e.tensor_single_scalar(
                            out=LTMP_PJ[:, ch * 1024:ch * 1024 + 640],
                            in_=WPF[qb % 2][:, ch * 640:(ch + 1) * 640],
                            scalar=thr_wp[:, 1:2], op=ALU.is_lt), waits=tw)
                        qv = p.emit("dve", lambda e, qb=qb, ch=ch:
                                    e.scalar_tensor_tensor(
                            out=WPQ[:, qb * 1280 + ch * 640:qb * 1280 + (ch + 1) * 640],
                            in0=WPF[qb % 2][:, ch * 640:(ch + 1) * 640],
                            scalar=thr_wp[:, 0:1],
                            in1=LTMP_PJ[:, ch * 1024:ch * 1024 + 640],
                            op0=ALU.is_gt, op1=ALU.subtract), inc="dve")
                    wpf_guard[qb % 2] = qv
                    tern_q[qb] = qv
                li = ei % 2
                lnw = [("st", s2["ln1_store"][ei]), ("dve", s2["ln1in_guard"][li])]
                lnv = p.emit("sync", lambda e, ei=ei, hoff=hoff, li=li: e.dma_start(
                    out=LN1IN[li][:, :],
                    in_=LNSCR[ei * 128:(ei + 1) * 128, hoff:hoff + HALF]),
                    waits=lnw, inc="hw", amt=16)
                dv = None
                for sc in range(2):
                    b = sc
                    v = None
                    for qg in range(4):
                        w = [("dve", tern_q[qg]), ("act", h1_last), ("hw", lnv)]
                        if qg == 0:
                            w += bank_wait(b)
                        def mmp(e, sc=sc, b=b, qg=qg):
                            r = None
                            for hh in range(qg * 10, qg * 10 + 10):
                                r = e.matmul(
                                    out=PS[b][:, 0:512],
                                    lhsT=WPQ[:, hh * 128:(hh + 1) * 128],
                                    rhs=H1[:, hh * 1024 + sc * 512:hh * 1024 + sc * 512 + 512],
                                    start=(hh == 0), stop=(hh == HT - 1))
                            return r
                        v = p.emit("pe", mmp, waits=w, inc="pe")
                        if sc == 1:
                            pq_done[qg] = v
                    dv = p.emit("dve", lambda e, ei=ei, sc=sc, b=b, li=li:
                                e.scalar_tensor_tensor(
                        out=zslot(ei)[:, sc * 512:(sc + 1) * 512], in0=PS[b][:, 0:512],
                        scalar=c_vec[:, 0:1], in1=LN1IN[li][:, sc * 512:(sc + 1) * 512],
                        op0=ALU.mult, op1=ALU.add), waits=[("pe", v)], inc="dve")
                    bank_free[b] = ("dve", dv)
                s2["ln1in_guard"][li] = dv
                zpar = ei % 2
                zbw = [("dve", dv), ("pe", s2["zbs_guard"][zpar])]
                p.emit("act", lambda e, ei=ei, zpar=zpar: e.copy(
                    out=ZBS[zpar], in_=zslot(ei)), waits=zbw, inc="act")
                sqv = p.emit("act", lambda e, ei=ei, zpar=zpar: e.square(
                    out=ZBS[2 + zpar], in_=zslot(ei)), inc="act")
                vlast = None
                for sc in range(2):
                    v = p.emit("pe", lambda e, sc=sc, ei=ei, zpar=zpar: e.matmul(
                        out=PS[2 + sc][0:1, 0:512], lhsT=ones_bcol[:, :],
                        rhs=ZBS[zpar][:, sc * 512:(sc + 1) * 512],
                        start=(ei == 0), stop=(ei == ET - 1), skip_group_check=True),
                        waits=[("act", sqv)], inc="pe")
                    v = p.emit("pe", lambda e, sc=sc, ei=ei, zpar=zpar: e.matmul(
                        out=PS[4 + sc][0:1, 0:512], lhsT=ones_bcol[:, :],
                        rhs=ZBS[2 + zpar][:, sc * 512:(sc + 1) * 512],
                        start=(ei == 0), stop=(ei == ET - 1), skip_group_check=True),
                        inc="pe")
                    vlast = v
                s2["zbs_guard"][zpar] = vlast
                if Z3DBG is not None:
                    p.emit("pool", lambda e, ei=ei, hoff=hoff: e.dma_start(
                        out=Z3DBG[ei * 128:(ei + 1) * 128, hoff:hoff + HALF],
                        in_=zslot(ei)), waits=[("act", sqv)], inc="sw", amt=16)
                if ei == ET - 1:
                    stats_stop2 = vlast
            s2["proj_end"] = p.c("pe")
            lnd2 = emit_ln_chain(stats_stop2, LNW_A, LNW_B)
            for ei in range(ET):
                p.emit("dve", lambda e, ei=ei: e.tensor_tensor(
                    out=zslot(ei), in0=zslot(ei), in1=LNW_A[:, :], op=ALU.mult),
                    waits=[("act", lnd2)])
                av = p.emit("dve", lambda e, ei=ei: e.tensor_tensor(
                    out=zslot(ei), in0=zslot(ei), in1=LNW_B[:, :], op=ALU.add),
                    inc="dve")
                s2["ln2_apply_last"] = av
                sv = p.emit("sync", lambda e, ei=ei, hoff=hoff: e.dma_start(
                    out=YT[ei * 128:(ei + 1) * 128, hoff:hoff + HALF], in_=zslot(ei)),
                    waits=[("dve", av)], inc="st", amt=16)
                s2["z_free"][ei] = sv
        if MDBG is not None:
            misc_sb = ctx.enter_context(nc.sbuf_tensor("misc_sb", [128, 16], F32))
            mv1 = p.emit("dve", lambda e: e.tensor_copy(out=misc_sb[:, 0:2], in_=thr_fc[:, :]),
                         waits=[("dve", p.c("dve")), ("act", p.c("act"))], inc="dve")
            mv1 = p.emit("dve", lambda e: e.tensor_copy(out=misc_sb[:, 2:4], in_=thr_wp[:, :]),
                         inc="dve")
            mv1 = p.emit("dve", lambda e: e.tensor_copy(out=misc_sb[:, 4:5], in_=c_vec[:, :]),
                         inc="dve")
            mv1 = p.emit("dve", lambda e: e.tensor_copy(out=misc_sb[0:1, 8:16], in_=rows[0:1, :]),
                         inc="dve")
            p.emit("sync", lambda e: e.dma_start(out=MDBG[:], in_=misc_sb[:]),
                   waits=[("dve", mv1)], inc="st", amt=16)
            for ii in range(4):
                p.emit("sync", lambda e, ii=ii: e.dma_start(
                    out=LDBG[ii * 512:(ii + 1) * 512, :],
                    in_=LNSCR[ii * 512:(ii + 1) * 512, :]), inc="st", amt=16)
        p.emit("sync", lambda e: e.nop(), waits=[("st", p.c("st"))])
    else:
        p.emit("sync", lambda e: e.nop(), waits=[("st", p.c("st"))])

    semnames = ["pe", "act", "dve", "hw", "st", "sw", "pl"]
    sems = {n: ctx.enter_context(nc.semaphore(n)) for n in semnames}

    def runner(ops):
        def go(eng):
            seen = {}
            for ws, fn, inc, amt in ops:
                for (s_, v_) in ws:
                    if seen.get(s_, 0) < v_:
                        eng.wait_ge(sems[s_], v_)
                        seen[s_] = v_
                inst = fn(eng)
                if inc is not None:
                    inst.then_inc(sems[inc], amt)
        return go

    with nc.Block() as blk:
        blk.tensor(runner(p.ops["pe"]))
        blk.scalar(runner(p.ops["act"]))
        blk.vector(runner(p.ops["dve"]))
        blk.sync(runner(p.ops["sync"]))
        blk.gpsimd(runner(p.ops["pool"]))

    ctx.close()
    return nc



_PERM = np.concatenate([np.arange(0, 128, 2), np.arange(1, 128, 2)])


def prep_weights(Wqkv, Wo, w_fc, w_proj):
    wq = np.empty((NH, E, 3 * D), np.float32)
    for h in range(NH):
        wq[h, :, 0:128] = Wqkv[:, h * 128:(h + 1) * 128][:, _PERM]
        wq[h, :, 128:256] = Wqkv[:, E + h * 128:E + (h + 1) * 128][:, _PERM]
        wq[h, :, 256:384] = Wqkv[:, 2 * E + h * 128:2 * E + (h + 1) * 128]
    wocb = np.ascontiguousarray(Wo.reshape(E, ET, 128).transpose(1, 0, 2))
    wfccb = np.ascontiguousarray(
        np.ascontiguousarray(w_fc.T).reshape(E, HT, 128).transpose(1, 0, 2))
    wpjcb = np.ascontiguousarray(
        np.ascontiguousarray(w_proj.T).reshape(HID, ET, 128).transpose(1, 0, 2))
    return wq, wocb, wfccb, wpjcb


def rope_tables():
    i = np.arange(64, dtype=np.float64)
    inv = 10000.0 ** (-2.0 * i / 128.0)
    pos = np.arange(S, dtype=np.float64)
    th = np.outer(inv, pos)
    c = np.cos(th).astype(np.float32)
    s = np.sin(th).astype(np.float32)
    cc = np.ascontiguousarray(np.concatenate([c, c], 0))
    ss = np.ascontiguousarray(np.concatenate([-s, s], 0))
    return cc, ss


def causal_mask():
    k = np.arange(128)[:, None]
    q = np.arange(128)[None, :]
    return np.where(k <= q, 0.0, MASKNEG).astype(np.float32)


def make_in_maps(x, Wqkv, Wo, w_fc, w_proj):
    wq, wocb, wfccb, wpjcb = prep_weights(Wqkv, Wo, w_fc, w_proj)
    cc, ss = rope_tables()
    msk = causal_mask()
    ident = np.eye(128, dtype=np.float32)
    in_maps = []
    for b in range(8):
        xt = np.ascontiguousarray(x[b].T)
        in_maps.append({
            "xt_in": xt, "wqkv_h": wq, "wo_cb": wocb, "wfc_cb": wfccb,
            "wpj_cb": wpjcb, "rope_cc": cc, "rope_ss": ss,
            "mask_in": msk, "ident_in": ident,
        })
    return in_maps


_NC_CACHE = {}


def kernel(L0_attn_x, L0_attn_Wqkv, L0_attn_Wo, L0_ffn_w_fc, L0_ffn_w_proj):
    x = np.asarray(L0_attn_x, np.float32)
    in_maps = make_in_maps(
        x, np.asarray(L0_attn_Wqkv, np.float32), np.asarray(L0_attn_Wo, np.float32),
        np.asarray(L0_ffn_w_fc, np.float32), np.asarray(L0_ffn_w_proj, np.float32))
    if "full" not in _NC_CACHE:
        _NC_CACHE["full"] = build_nc()
    nc = _NC_CACHE["full"]
    res = run_bass_kernel_spmd(nc, in_maps, core_ids=list(range(8)))
    return np.stack([np.asarray(r["yt_out"]).T for r in res.results], 0)


# version 14
# speedup vs baseline: 1.5815x; 1.5815x over previous
import math
import contextlib
import numpy as np
import ml_dtypes
import concourse.bass as bass
import concourse.mybir as mybir
from concourse.bass_utils import run_bass_kernel_spmd

F32 = mybir.dt.float32
BF16 = mybir.dt.bfloat16
AF = mybir.ActivationFunctionType
ALU = mybir.AluOpType
AX = mybir.AxisListType

S = 2048
E = 2048
NH = 16
D = 128
HID = 5120
ET = E // 128
HT = HID // 128
NEG_BIAS = -8.0
MASKNEG = -1.0e9
INV_SQRT_D = 1.0 / math.sqrt(D)
N_FC = float(HID * E)
N_WP = float(E * HID)
HALF = 1024

ARENA_F32_COLS = 47616


class Prog:
    ENGS = ("pe", "act", "dve", "sync", "pool")

    def __init__(self):
        self.ops = {e: [] for e in self.ENGS}
        self.cnt = {}

    def emit(self, eng, fn, waits=(), inc=None, amt=1):
        wd = {}
        for (s, v) in waits:
            if v is not None and v > 0:
                wd[s] = max(wd.get(s, 0), int(v))
        ws = list(wd.items())
        self.ops[eng].append((ws, fn, inc, amt))
        if inc is not None:
            self.cnt[inc] = self.cnt.get(inc, 0) + amt
            return self.cnt[inc]
        return None

    def c(self, sem):
        return self.cnt.get(sem, 0)


def build_nc(n_heads=NH, stage2=True, dbg=None):
    nc = bass.Bass("TRN2", target_bir_lowering=False)

    XT = nc.dram_tensor("xt_in", [E, S], F32, kind="ExternalInput").ap()
    WQKV = nc.dram_tensor("wqkv_h", [NH, 128, ET * 384], F32, kind="ExternalInput").ap()
    WOCB = nc.dram_tensor("wo_cb", [ET, 128, ET * 128], F32, kind="ExternalInput").ap()
    WFCCB = nc.dram_tensor("wfc_cb", [HT, 2, 128, 1024], F32, kind="ExternalInput").ap()
    WPJCB = nc.dram_tensor("wpj_cb", [ET, 4, 128, 1280], F32, kind="ExternalInput").ap()
    TCC = nc.dram_tensor("rope_cc", [128, S], F32, kind="ExternalInput").ap()
    TSS = nc.dram_tensor("rope_ss", [128, S], F32, kind="ExternalInput").ap()
    MASKIN = nc.dram_tensor("mask_in", [128, 128], F32, kind="ExternalInput").ap()
    IDIN = nc.dram_tensor("ident_in", [128, 128], F32, kind="ExternalInput").ap()
    YT = nc.dram_tensor("yt_out", [E, S], F32, kind="ExternalOutput").ap()
    OSCR = nc.dram_tensor("o_scr", [NH, 128, S], BF16, kind="Internal").ap()
    LNSCR = nc.dram_tensor("ln1_scr", [E, S], F32, kind="Internal").ap()
    ODBG = None
    if dbg == "oscr":
        ODBG = nc.dram_tensor("o_dbg", [NH, 128, S], F32, kind="ExternalOutput").ap()
    ZDBG = MDBG = LDBG = O2DBG = Z3DBG = HDBG = ZBDBG = WQDBG = None
    if dbg == "s2":
        ZDBG = nc.dram_tensor("z_dbg", [E, S], F32, kind="ExternalOutput").ap()
        MDBG = nc.dram_tensor("m_dbg", [128, 16], F32, kind="ExternalOutput").ap()
        LDBG = nc.dram_tensor("l_dbg", [E, S], F32, kind="ExternalOutput").ap()
        O2DBG = nc.dram_tensor("o2_dbg", [128, 16384], F32, kind="ExternalOutput").ap()
        Z3DBG = nc.dram_tensor("z3_dbg", [E, S], F32, kind="ExternalOutput").ap()
        HDBG = nc.dram_tensor("h_dbg", [128, 4096], F32, kind="ExternalOutput").ap()
        ZBDBG = nc.dram_tensor("zb_dbg", [128, 16384], F32, kind="ExternalOutput").ap()
        WQDBG = nc.dram_tensor("wq_dbg", [128, 2048], F32, kind="ExternalOutput").ap()

    p = Prog()
    ctx = contextlib.ExitStack()

    RAW = ctx.enter_context(nc.sbuf_tensor("arena", [128, ARENA_F32_COLS], F32))
    RAWB = RAW[:].bitcast(BF16)

    def f32(off, cols):
        return RAW[:, off:off + cols]

    def bf(off, cols):
        return RAWB[:, 2 * off:2 * off + cols]

    cc_sb = ctx.enter_context(nc.sbuf_tensor("cc_sb", [128, S], BF16))
    ss_sb = ctx.enter_context(nc.sbuf_tensor("ss_sb", [128, S], BF16))
    mask_sb = ctx.enter_context(nc.sbuf_tensor("mask_sb", [128, 128], BF16))
    id_sb = ctx.enter_context(nc.sbuf_tensor("id_sb", [128, 128], BF16))
    ones_bcol = ctx.enter_context(nc.sbuf_tensor("ones_bcol", [128, 1], BF16))
    ones_fcol = ctx.enter_context(nc.sbuf_tensor("ones_fcol", [128, 1], F32))
    ones_frow = ctx.enter_context(nc.sbuf_tensor("ones_frow", [1, 128], F32))
    acc_fc = ctx.enter_context(nc.sbuf_tensor("acc_fc", [128, 80], F32))
    acc_wp = ctx.enter_context(nc.sbuf_tensor("acc_wp", [128, 80], F32))
    accT = ctx.enter_context(nc.sbuf_tensor("accT", [128, 2], F32))
    rows = ctx.enter_context(nc.sbuf_tensor("rows_sm", [1, 8], F32))
    thr_fc = ctx.enter_context(nc.sbuf_tensor("thr_fc", [128, 2], F32))
    thr_wp = ctx.enter_context(nc.sbuf_tensor("thr_wp", [128, 2], F32))
    c_vec = ctx.enter_context(nc.sbuf_tensor("c_vec", [128, 1], F32))
    rl_sb = ctx.enter_context(nc.sbuf_tensor("rl_sb", [128, 4], F32))
    consts = ctx.enter_context(nc.sbuf_tensor("consts", [128, 4], F32))
    lnrows = ctx.enter_context(nc.sbuf_tensor("lnrows", [1, 2560], F32))
    LR = {"m": lnrows[0:1, 0:512], "v": lnrows[0:1, 512:1024],
          "s": lnrows[0:1, 1024:1536], "A": lnrows[0:1, 1536:2048],
          "B": lnrows[0:1, 2048:2560]}

    PS = [ctx.enter_context(nc.psum_tensor(f"ps{i}", [128, 512], F32)) for i in range(8)]
    PSB = [PS[i][:].bitcast(BF16) for i in range(8)]

    o_Xb = 0
    o_WH0, o_WH1 = 16384, 19456
    o_Qd0, o_Qd1 = 22528, 23552
    o_Kd0, o_Kd1 = 24576, 25600
    o_Vd = 26624
    o_Vt0, o_Vt1 = 27648, 28736
    o_PT0, o_PT1 = 29824, 33920
    o_Ot = 38016
    o_Ofm = 38528
    o_Ff = 39552
    o_Fs = 41600
    o_tT = 43648
    o_tU = 44160
    o_MB = 44672
    assert 47232 <= ARENA_F32_COLS
    Xb = bf(o_Xb, 32768)
    WH = [bf(o_WH0, 6144), bf(o_WH1, 6144)]
    Qd = [bf(o_Qd0, 2048), bf(o_Qd1, 2048)]
    Kd = [bf(o_Kd0, 2048), bf(o_Kd1, 2048)]
    Vd = bf(o_Vd, 2048)
    Vt = [bf(o_Vt0, 2176), bf(o_Vt1, 2176)]
    PT = [bf(o_PT0, 8192), bf(o_PT1, 8192)]
    Ot = bf(o_Ot, 1024)
    Ofm = bf(o_Ofm, 2048)
    Ff = f32(o_Ff, 2048)
    Fs = f32(o_Fs, 2048)
    tT = f32(o_tT, 512)
    tU = f32(o_tU, 512)
    MBUF = [bf(o_MB + k * 640, 1280) for k in range(4)]

    o_Z = 0
    o_H1 = 16384
    o_T0 = 36864
    o_T1 = 39936
    o_TL = 42496
    o_T3 = 44544
    o_T4 = 45568
    assert 45568 + 2048 <= ARENA_F32_COLS
    Z = RAW[:, o_Z:o_Z + 16384]
    ZB = RAWB[:, 2 * o_Z:2 * o_Z + 32768]
    H1 = bf(o_H1, 40960)
    O2 = bf(o_H1, 16384)
    WoCB = [bf(o_H1 + 8192, 2048), bf(o_H1 + 9216, 2048)]
    LNW_A = f32(o_H1 + 10240, 1024)
    LNW_B = f32(o_H1 + 11264, 1024)
    LN1IN = [f32(o_TL, 1024), f32(o_TL + 1024, 1024)]
    WPQ = bf(o_T0, 5120)
    WFCQ = [bf(o_T0, 2048), bf(o_T0 + 1024, 2048)]
    LTMP_FC = bf(o_T0 + 2048, 2048)
    WPF = [f32(o_T1, 1280), f32(o_T1 + 1280, 1280)]
    WFCF = [f32(o_T1, 1024), f32(o_T1 + 1024, 1024)]
    HREL = [f32(o_T3, 512), f32(o_T3 + 512, 512)]
    LTMP_PJ = bf(o_T3, 2048)
    ZBS = [bf(o_T4, 1024), bf(o_T4 + 512, 1024), bf(o_T4 + 1024, 1024),
           bf(o_T4 + 1536, 1024)]

    def zslot(i):
        return Z[:, i * 1024:(i + 1) * 1024]

    bank_free = {}

    def bank_wait(i):
        w = bank_free.get(i)
        return [w] if w else []

    p.emit("pool", lambda e: e.dma_start(out=cc_sb[:], in_=TCC[:]), inc="sw", amt=16)
    p.emit("pool", lambda e: e.dma_start(out=ss_sb[:], in_=TSS[:]), inc="sw", amt=16)
    v_tabs = p.emit("pool", lambda e: e.dma_start(out=id_sb[:], in_=IDIN[:]),
                    inc="sw", amt=16)
    v_mask = p.emit("pool", lambda e: e.dma_start(out=mask_sb[:], in_=MASKIN[:]),
                    inc="sw", amt=16)

    XTr = XT.rearrange("(et pp) s -> pp et s", pp=128)
    v_xb = None
    for c4 in range(4):
        v_xb = p.emit("pool", lambda e, c4=c4: e.dma_start(
            out=Xb[:].rearrange("pp (et s) -> pp et s", s=2048)[:, c4 * 4:(c4 + 1) * 4, :],
            in_=XTr[:, c4 * 4:(c4 + 1) * 4, :]), inc="sw", amt=16)

    def load_wh(h, waits):
        par = h % 2
        return p.emit("pool", lambda e, h=h, par=par: e.dma_start(
            out=WH[par][:, :], in_=WQKV[h]), waits=waits, inc="sw", amt=16)

    v_wh = [None] * (NH + 1)
    v_wh[0] = load_wh(0, [])

    p.emit("dve", lambda e: e.memset(consts[:, 0:1], NEG_BIAS))
    p.emit("dve", lambda e: e.memset(consts[:, 1:2], INV_SQRT_D))
    p.emit("dve", lambda e: e.memset(consts[:, 2:3], -0.5))
    v_consts = p.emit("dve", lambda e: e.memset(consts[:, 3:4], 1e-5), inc="dve")
    p.emit("dve", lambda e: e.memset(ones_bcol[:, :], 1.0))
    p.emit("dve", lambda e: e.memset(ones_fcol[:, :], 1.0))
    p.emit("dve", lambda e: e.memset(ones_frow[:, :], 1.0))
    p.emit("dve", lambda e: e.memset(acc_fc[:, :], 0.0))
    p.emit("dve", lambda e: e.memset(acc_wp[:, :], 0.0))
    mzv = None
    for par in range(2):
        vt_v = Vt[par].rearrange("pp (kt c) -> pp kt c", c=136)
        mzv = p.emit("dve", lambda e, vt_v=vt_v: e.memset(vt_v[:, :, 128:136], 1.0),
                     inc="dve")

    st = {
        "qkv_last": [None] * NH,
        "rope_done": [None] * NH,
        "pv_prev": [None, None],
        "st_prev": [None, None],
        "strip_guard": [None, None],
        "ot_guard": None,
        "ofm_guard": None,
        "rl_guard": [None, None],
        "mb_guard": [None, None, None, None],
        "mean_lv": [None, None, None, None],
        "mean_pos": 0,
        "vtr_prev": None,
        "vt_ev": [[None, None], [None, None]],
        "mean_fc": None,
        "mean_wp": None,
        "did_fc_chain": False,
        "did_wp_chain": False,
        "trb": 6,
    }

    def next_trb():
        b = st["trb"]
        st["trb"] = 13 - b
        return b

    st["evac"] = {}
    st["rope_q"] = None
    A = {}

    def emit_qkv_tensor(h, t):
        par = h % 2
        toff = t * 128
        for sc in range(4):
            b = next_trb()
            waits = [("sw", v_xb), ("sw", v_wh[h])] + bank_wait(b)
            def mmq(e, par=par, toff=toff, sc=sc, b=b):
                r = None
                for et in range(ET):
                    r = e.matmul(
                        out=PS[b][:, 0:512],
                        lhsT=WH[par][:, et * 384 + toff:et * 384 + toff + 128],
                        rhs=Xb[:, et * 2048 + sc * 512:et * 2048 + (sc + 1) * 512],
                        start=(et == 0), stop=(et == ET - 1))
                return r
            v = p.emit("pe", mmq, waits=waits, inc="pe")
            if t < 2:
                w2 = [("pe", v)]
                if sc == 0 and t == 0 and h >= 1:
                    w2.append(("dve", st["rope_done"][h - 1]))
                if sc == 0 and t == 1:
                    w2.append(("dve", st["rope_q"]))
                ev = p.emit("act", lambda e, sc=sc, b=b: e.copy(
                    out=Ff[:, sc * 512:(sc + 1) * 512], in_=PS[b][:, 0:512]),
                    waits=w2, inc="act")
            else:
                w2 = [("pe", v)]
                if sc == 0:
                    w2.append(("pe", st["vtr_prev"]))
                ev = p.emit("act", lambda e, sc=sc, b=b: e.copy(
                    out=Vd[:, sc * 512:(sc + 1) * 512], in_=PS[b][:, 0:512]),
                    waits=w2, inc="act")
            st["evac"][(t, sc)] = ev
            bank_free[b] = ("act", ev)
        if t < 2:
            wsw = [("act", st["evac"][(t, 3)])]
            if t == 0 and h >= 1:
                wsw.append(("dve", st["rope_done"][h - 1]))
            else:
                wsw.append(("dve", st["rope_q"]))
            p.emit("sync", lambda e: e.dma_start(out=Fs[0:64, :], in_=Ff[64:128, :]),
                   waits=wsw, inc="hw", amt=16)
            swv = p.emit("sync", lambda e: e.dma_start(out=Fs[64:128, :], in_=Ff[0:64, :]),
                         inc="hw", amt=16)
            dstd = Qd[par] if t == 0 else Kd[par]
            rv = None
            for sc in range(4):
                w3 = []
                if sc == 0:
                    w3 = [("hw", swv), ("pe", st["st_prev"][par]), ("sw", v_tabs)]
                p.emit("dve", lambda e, sc=sc: e.tensor_tensor(
                    out=tT[:, :], in0=Ff[:, sc * 512:(sc + 1) * 512],
                    in1=cc_sb[:, sc * 512:(sc + 1) * 512], op=ALU.mult), waits=w3)
                p.emit("dve", lambda e, sc=sc: e.tensor_tensor(
                    out=tU[:, :], in0=Fs[:, sc * 512:(sc + 1) * 512],
                    in1=ss_sb[:, sc * 512:(sc + 1) * 512], op=ALU.mult))
                rv = p.emit("dve", lambda e, sc=sc, dstd=dstd: e.tensor_tensor(
                    out=dstd[:, sc * 512:(sc + 1) * 512], in0=tT[:, :], in1=tU[:, :],
                    op=ALU.add), inc="dve")
            if t == 0:
                st["rope_q"] = rv
            else:
                st["rope_done"][h] = rv

    def emit_vtrans(h):
        par = h % 2
        vlast = None
        for bt in range(2):
            b = next_trb()
            waits = [("act", st["evac"][(2, 2 * bt + 1)]), ("sw", v_tabs)] + bank_wait(b)
            def vtr(e, bt=bt, b=b):
                r = None
                for j in range(8):
                    si = bt * 8 + j
                    r = e.transpose(out=PSB[b][:, j * 128:(j + 1) * 128],
                                    in_=Vd[:, si * 128:(si + 1) * 128],
                                    identity=id_sb[:, :])
                return r
            v = p.emit("pe", vtr, waits=waits, inc="pe")
            vlast = v
            vt_v = Vt[par].rearrange("pp (kt c) -> pp kt c", c=136)
            w4 = [("pe", v), ("dve", mzv), ("pe", st["pv_prev"][par])]
            ev = p.emit("dve", lambda e, bt=bt, b=b, vt_v=vt_v: e.tensor_copy(
                out=vt_v[:, bt * 8:(bt + 1) * 8, 0:128], in_=PSB[b][:, 0:1024]),
                waits=w4, inc="dve")
            st["vt_ev"][par][bt] = ev
            bank_free[b] = ("dve", ev)
        st["vtr_prev"] = vlast
        st["qkv_last"][h] = p.c("pe")

    N_MCH = 2 * HT + 64

    def mean_chunk_src(ci):
        if ci < 2 * HT:
            return WFCCB[ci // 2, ci % 2], 1024
        cj = ci - 2 * HT
        return WPJCB[cj // 4, cj % 4], 1280

    def mean_emit_load(ci):
        srcv, cols = mean_chunk_src(ci)
        buf = MBUF[ci % 4]
        lw = []
        if st["mb_guard"][ci % 4] is not None:
            lw.append(st["mb_guard"][ci % 4])
        st["mean_lv"][ci % 4] = p.emit(
            "pool", lambda e, buf=buf, srcv=srcv, cols=cols: e.dma_start(
                out=buf[:, 0:cols], in_=srcv), waits=lw, inc="sw", amt=16)

    def mean_emit_red(ci):
        srcv, cols = mean_chunk_src(ci)
        buf = MBUF[ci % 4]
        if ci < 2 * HT:
            acc, col = acc_fc, ci
        else:
            acc, col = acc_wp, ci - 2 * HT
        lv = st["mean_lv"][ci % 4]
        rv = p.emit("dve", lambda e, acc=acc, col=col, cols=cols, buf=buf:
                    e.tensor_reduce(
            out=acc[:, col:col + 1], in_=buf[:, 0:cols],
            axis=AX.X, op=ALU.add, apply_absolute_value=True),
            waits=[("sw", lv)], inc="dve")
        st["mb_guard"][ci % 4] = ("dve", rv)
        if ci == 2 * HT - 1:
            st["mean_fc"] = ("dve", rv)
        if ci == N_MCH - 1:
            st["mean_wp"] = ("dve", rv)

    mean_plan = [("load", ci) for ci in range(4)]
    for ci in range(N_MCH):
        mean_plan.append(("red", ci))
        if ci + 4 < N_MCH:
            mean_plan.append(("load", ci + 4))

    def mean_tick(h, n):
        for _ in range(n):
            pos = st["mean_pos"]
            if pos >= len(mean_plan):
                return
            st["mean_pos"] += 1
            kind, ci = mean_plan[pos]
            if kind == "load":
                mean_emit_load(ci)
            else:
                mean_emit_red(ci)

    def emit_thr_chain(which):
        acc = acc_fc if which == 0 else acc_wp
        nel = N_FC if which == 0 else N_WP
        done = st["mean_fc"] if which == 0 else st["mean_wp"]
        rv = p.emit("dve", lambda e: e.tensor_reduce(
            out=accT[:, which:which + 1], in_=acc[:, 0:80], axis=AX.X, op=ALU.add),
            waits=[done], inc="dve")
        b = 7
        v = p.emit("pe", lambda e: e.matmul(
            out=PS[b][0:1, 0:1], lhsT=ones_fcol[:, :], rhs=accT[:, which:which + 1],
            start=True, stop=True), waits=[("dve", rv)] + bank_wait(b), inc="pe")
        av = p.emit("dve", lambda e: e.tensor_scalar(
            out=rows[0:1, which:which + 1], in0=PS[b][0:1, 0:1],
            scalar1=1.0 / nel, scalar2=1e-5, op0=ALU.mult, op1=ALU.max),
            waits=[("pe", v)], inc="dve")
        bank_free[b] = ("dve", av)
        tv = p.emit("dve", lambda e: e.tensor_single_scalar(
            out=rows[0:1, 2 + which:3 + which], in_=rows[0:1, which:which + 1],
            scalar=0.7, op=ALU.mult), waits=[("dve", av)], inc="dve")
        tvec = thr_fc if which == 0 else thr_wp
        v2 = p.emit("pe", lambda e: e.matmul(
            out=PS[b][0:128, 0:1], lhsT=ones_frow[:, :],
            rhs=rows[0:1, 2 + which:3 + which], start=True, stop=True),
            waits=[("dve", tv)] + bank_wait(b), inc="pe")
        c1 = p.emit("act", lambda e, tvec=tvec: e.copy(
            out=tvec[:, 0:1], in_=PS[b][0:128, 0:1]), waits=[("pe", v2)], inc="act")
        e2 = p.emit("dve", lambda e, tvec=tvec: e.tensor_single_scalar(
            out=tvec[:, 1:2], in_=PS[b][0:128, 0:1], scalar=-1.0, op=ALU.mult),
            waits=[("pe", v2), ("act", c1)], inc="dve")
        bank_free[b] = ("dve", e2)
        if which == 1:
            s1 = p.emit("dve", lambda e: e.tensor_tensor(
                out=rows[0:1, 4:5], in0=rows[0:1, 0:1], in1=rows[0:1, 0:1], op=ALU.mult),
                waits=[("dve", av)], inc="dve")
            s2 = p.emit("dve", lambda e: e.tensor_tensor(
                out=rows[0:1, 5:6], in0=rows[0:1, 4:5], in1=rows[0:1, 1:2], op=ALU.mult),
                waits=[("dve", s1)], inc="dve")
            v3 = p.emit("pe", lambda e: e.matmul(
                out=PS[b][0:128, 4:5], lhsT=ones_frow[:, :], rhs=rows[0:1, 5:6],
                start=True, stop=True), waits=[("dve", s2)] + bank_wait(b), inc="pe")
            e3 = p.emit("act", lambda e: e.copy(out=c_vec[:, 0:1], in_=PS[b][0:128, 4:5]),
                        waits=[("pe", v3)], inc="act")
            bank_free[b] = ("act", e3)

    def attn_begin(hh):
        A.clear()
        A["stb"] = 0
        A["exp"] = {}
        A["pv"] = {}
        A["maskv"] = {}
        A["stlast"] = None

    def emit_attn_group(hh, g):
        par = hh % 2
        pg = g % 2
        exp_done = A["exp"]
        pv_qt = A["pv"]
        for kt in range(4 * g + 4):
            c0 = max(0, kt * 128 - g * 512)
            b = A["stb"]
            A["stb"] = (b + 1) % 4
            w = [("dve", st["rope_done"][hh])] + bank_wait(b)
            v = p.emit("pe", lambda e, par=par, g=g, kt=kt, c0=c0, b=b: e.matmul(
                out=PS[b][:, c0:512],
                lhsT=Kd[par][:, kt * 128:(kt + 1) * 128],
                rhs=Qd[par][:, g * 512 + c0:(g + 1) * 512],
                start=True, stop=True), waits=w, inc="pe")
            A["stlast"] = v
            wexp = [("pe", v), ("pe", st["strip_guard"][pg]), ("dve", v_consts)]
            ev = p.emit("act", lambda e, pg=pg, kt=kt, c0=c0, b=b: e.activation(
                out=PT[pg][:, kt * 512 + c0:(kt + 1) * 512], in_=PS[b][:, c0:512],
                func=AF.Exp, bias=consts[:, 0:1], scale=consts[:, 1:2]),
                waits=wexp, inc="act")
            exp_done[(g, kt)] = ev
            bank_free[b] = ("act", ev)
            if kt >= 4 * g:
                mv = p.emit("pool", lambda e, pg=pg, kt=kt, c0=c0: e.tensor_tensor(
                    out=PT[pg][:, kt * 512 + c0:kt * 512 + c0 + 128],
                    in0=PT[pg][:, kt * 512 + c0:kt * 512 + c0 + 128],
                    in1=mask_sb[:, :], op=ALU.mult),
                    waits=[("act", ev), ("sw", v_mask)], inc="pl")
                A["maskv"][kt] = mv
        for qt in range(4 * g, 4 * g + 4):
            ob = 4 + qt % 2
            w = ([("act", exp_done[(g, qt)]), ("pl", A["maskv"][qt]),
                  ("dve", st["vt_ev"][par][1])] + bank_wait(ob))
            def pv(e, par=par, pg=pg, g=g, qt=qt, ob=ob):
                r = None
                for kt in range(qt + 1):
                    co = kt * 512 + qt * 128 - g * 512
                    r = e.matmul(out=PS[ob][:, 0:129],
                                 lhsT=PT[pg][:, co:co + 128],
                                 rhs=Vt[par][:, kt * 136:kt * 136 + 129],
                                 start=(kt == 0), stop=(kt == qt))
                return r
            v = p.emit("pe", pv, waits=w, inc="pe")
            pv_qt[qt] = v
            wr = [("pe", v), ("act", st["rl_guard"][qt % 2])]
            av = p.emit("act", lambda e, ob=ob, qt=qt: e.activation(
                out=rl_sb[:, 2 + qt % 2:3 + qt % 2], in_=PS[ob][:, 128:129],
                func=AF.Ln), waits=wr, inc="act")
            rv = p.emit("act", lambda e, qt=qt: e.activation(
                out=rl_sb[:, qt % 2:qt % 2 + 1],
                in_=rl_sb[:, 2 + qt % 2:3 + qt % 2],
                func=AF.Exp, scale=-1.0), inc="act")
            wo = [("act", rv)]
            if qt % 8 == 0:
                wo.append(("pe", st["ot_guard"]))
            ev = p.emit("act", lambda e, ob=ob, qt=qt: e.activation(
                out=Ot[:, (qt % 8) * 128:(qt % 8 + 1) * 128], in_=PS[ob][:, 0:128],
                func=AF.Copy, scale=rl_sb[:, qt % 2:qt % 2 + 1]),
                waits=wo, inc="act")
            st["rl_guard"][qt % 2] = ev
            bank_free[ob] = ("act", ev)
        st["strip_guard"][pg] = pv_qt[4 * g + 3]
        if g % 2 == 1:
            bb = g // 2
            b = next_trb()
            w = ([("act", st["rl_guard"][0]), ("act", st["rl_guard"][1])]
                 + bank_wait(b))
            def otr(e, b=b):
                r = None
                for j in range(8):
                    r = e.transpose(out=PSB[b][:, j * 128:(j + 1) * 128],
                                    in_=Ot[:, j * 128:(j + 1) * 128],
                                    identity=id_sb[:, :])
                return r
            v = p.emit("pe", otr, waits=w, inc="pe")
            st["ot_guard"] = v
            wf = [("pe", v)]
            if bb == 0:
                wf.append(("st", st["ofm_guard"]))
            ev = p.emit("dve", lambda e, bb=bb, b=b: e.tensor_copy(
                out=Ofm[:, bb * 1024:(bb + 1) * 1024], in_=PSB[b][:, 0:1024]),
                waits=wf, inc="dve")
            bank_free[b] = ("dve", ev)
            if bb == 1:
                st["ofm_guard"] = p.emit("sync", lambda e, hh=hh: e.dma_start(
                    out=OSCR[hh], in_=Ofm[:, :]), waits=[("dve", ev)],
                    inc="st", amt=16)
                if ODBG is not None:
                    p.emit("pool", lambda e, hh=hh: e.dma_start(
                        out=ODBG[hh], in_=Ofm[:, :]),
                        waits=[("dve", ev)], inc="st", amt=16)
        if g == 3:
            st["pv_prev"][par] = pv_qt[15]
            st["st_prev"][par] = A["stlast"]

    for h in range(n_heads + 1):
        a = h >= 1
        q = h < n_heads
        if a:
            attn_begin(h - 1)
            emit_attn_group(h - 1, 0)
        if q:
            if h + 1 < n_heads:
                wv = [("pe", st["qkv_last"][h - 1])] if h >= 1 else []
                v_wh[h + 1] = load_wh(h + 1, wv)
            emit_qkv_tensor(h, 0)
        if a:
            emit_attn_group(h - 1, 1)
        if q:
            emit_qkv_tensor(h, 1)
            if stage2:
                mean_tick(h, 5)
        if a:
            emit_attn_group(h - 1, 2)
        if q and stage2:
            mean_tick(h, 5)
        if q:
            emit_qkv_tensor(h, 2)
            emit_vtrans(h)
            if stage2:
                mean_tick(h, 4)
        if a:
            emit_attn_group(h - 1, 3)
        if q and stage2:
            mean_tick(h, 5)
            if st["mean_fc"] is not None and not st["did_fc_chain"]:
                emit_thr_chain(0)
                st["did_fc_chain"] = True
            if st["mean_wp"] is not None and not st["did_wp_chain"]:
                emit_thr_chain(1)
                st["did_wp_chain"] = True

    if stage2:
        BAR = [("pe", p.c("pe")), ("act", p.c("act")), ("dve", p.c("dve")),
               ("st", p.c("st")), ("hw", p.c("hw")), ("sw", p.c("sw")),
               ("pl", p.c("pl"))]
        for eng in ("pe", "act", "dve", "sync", "pool"):
            own = {"pe": "pe", "act": "act", "dve": "dve", "sync": None,
                   "pool": "pl"}[eng]
            w = [x for x in BAR if x[0] != own]
            p.emit(eng, lambda e: e.nop(), waits=w)

        s2 = {
            "z_free": [None] * ET,
            "z2b_cast": [None] * ET,
            "ln1_store": [None] * ET,
            "ln1_apply_last": None,
            "ln2_apply_last": None,
            "t0_guard": None,
            "t3_guard": None,
            "ln1in_guard": [None, None],
            "zbs_guard": [None, None],
            "wpf_guard": [None, None],
            "pq_done": [None] * 4,
            "wo_phase_end": None,
            "proj_end": None,
        }

        def emit_ln_chain(stats_stop, dst_A, dst_B):
            last = None
            for sc in range(2):
                sb_, qb_ = 2 + sc, 4 + sc
                p.emit("dve", lambda e, sb_=sb_: e.tensor_single_scalar(
                    out=LR["m"], in_=PS[sb_][0:1, 0:512], scalar=1.0 / E, op=ALU.mult),
                    waits=[("pe", stats_stop)])
                p.emit("dve", lambda e, qb_=qb_: e.tensor_single_scalar(
                    out=LR["v"], in_=PS[qb_][0:1, 0:512], scalar=1.0 / E, op=ALU.mult))
                p.emit("dve", lambda e: e.tensor_tensor(
                    out=LR["s"], in0=LR["m"], in1=LR["m"], op=ALU.mult))
                dv = p.emit("dve", lambda e: e.tensor_tensor(
                    out=LR["v"], in0=LR["v"], in1=LR["s"], op=ALU.subtract), inc="dve")
                av = p.emit("act", lambda e: e.activation(
                    out=LR["v"], in_=LR["v"], func=AF.Ln, bias=consts[0:1, 3:4]),
                    waits=[("dve", dv)], inc="act")
                av = p.emit("act", lambda e: e.activation(
                    out=LR["A"], in_=LR["v"], func=AF.Exp, scale=consts[0:1, 2:3]),
                    waits=[("act", av)], inc="act")
                bv = p.emit("dve", lambda e: e.scalar_tensor_tensor(
                    out=LR["B"], in0=LR["m"], scalar=-1.0, in1=LR["A"],
                    op0=ALU.mult, op1=ALU.mult), waits=[("act", av)], inc="dve")
                for (bk, src, dst) in ((6, LR["A"], dst_A), (7, LR["B"], dst_B)):
                    v1 = p.emit("pe", lambda e, bk=bk, src=src: e.matmul(
                        out=PS[bk][0:128, 0:512], lhsT=ones_frow[:, :], rhs=src,
                        start=True, stop=True), waits=[("dve", bv)] + bank_wait(bk),
                        inc="pe")
                    e1 = p.emit("act", lambda e, sc=sc, bk=bk, dst=dst: e.copy(
                        out=dst[:, sc * 512:(sc + 1) * 512], in_=PS[bk][0:128, 0:512]),
                        waits=[("pe", v1)], inc="act")
                    bank_free[bk] = ("act", e1)
                    last = e1
            for bk in (2, 3, 4, 5):
                bank_free[bk] = ("act", last)
            return last

        def make_stats(ei, sqv):
            zpar = ei % 2

            def go():
                vlast = None
                for sc in range(2):
                    v = p.emit("pe", lambda e, sc=sc, ei=ei, zpar=zpar: e.matmul(
                        out=PS[2 + sc][0:1, 0:512], lhsT=ones_bcol[:, :],
                        rhs=ZBS[zpar][:, sc * 512:(sc + 1) * 512],
                        start=(ei == 0), stop=(ei == ET - 1), skip_group_check=True),
                        waits=[("act", sqv)], inc="pe")
                    v = p.emit("pe", lambda e, sc=sc, ei=ei, zpar=zpar: e.matmul(
                        out=PS[4 + sc][0:1, 0:512], lhsT=ones_bcol[:, :],
                        rhs=ZBS[2 + zpar][:, sc * 512:(sc + 1) * 512],
                        start=(ei == 0), stop=(ei == ET - 1), skip_group_check=True),
                        inc="pe")
                    vlast = v
                s2["zbs_guard"][zpar] = vlast
                return vlast
            return go

        st_oscr = p.c("st")
        for half in range(2):
            hoff = half * HALF
            o2w = [("st", st_oscr)]
            if half == 1:
                o2w.append(("pe", s2["proj_end"]))
            o2v = p.emit("sync", lambda e, hoff=hoff: e.dma_start(
                out=O2[:].rearrange("pp (et s) -> pp et s", s=1024),
                in_=OSCR[:, :, hoff:hoff + HALF].rearrange("et pp s -> pp et s")),
                waits=o2w, inc="hw", amt=16)
            if O2DBG is not None and half == 0:
                p.emit("pool", lambda e: e.dma_start(out=O2DBG[:], in_=O2[:, :]),
                       waits=[("hw", o2v)], inc="sw", amt=16)
            wo_mm_done = [None, None]
            stats_stop = None
            pend_stats = None
            for ei in range(ET):
                parw = ei % 2
                wv = [("pe", wo_mm_done[parw])] if ei >= 2 else []
                if ei < 2 and half == 1:
                    wv.append(("pe", s2["proj_end"]))
                wcv = p.emit("pool", lambda e, ei=ei, parw=parw: e.dma_start(
                    out=WoCB[parw][:, :], in_=WOCB[ei]), waits=wv, inc="sw", amt=16)
                xw = [("st", s2["z_free"][ei])] if s2["z_free"][ei] else []
                xv = p.emit("sync", lambda e, ei=ei, hoff=hoff: e.dma_start(
                    out=zslot(ei), in_=XT[ei * 128:(ei + 1) * 128, hoff:hoff + HALF]),
                    waits=xw, inc="hw", amt=16)
                zadd = None
                for sc in range(2):
                    b = sc
                    w = [("sw", wcv), ("hw", o2v)] + bank_wait(b)
                    def mmw(e, parw=parw, sc=sc, b=b):
                        r = None
                        for et in range(ET):
                            r = e.matmul(
                                out=PS[b][:, 0:512],
                                lhsT=WoCB[parw][:, et * 128:(et + 1) * 128],
                                rhs=O2[:, et * 1024 + sc * 512:et * 1024 + sc * 512 + 512],
                                start=(et == 0), stop=(et == ET - 1))
                        return r
                    v = p.emit("pe", mmw, waits=w, inc="pe")
                    zadd = p.emit("dve", lambda e, ei=ei, sc=sc, b=b: e.tensor_tensor(
                        out=zslot(ei)[:, sc * 512:(sc + 1) * 512],
                        in0=zslot(ei)[:, sc * 512:(sc + 1) * 512],
                        in1=PS[b][:, 0:512], op=ALU.add),
                        waits=[("pe", v), ("hw", xv)], inc="dve")
                    bank_free[b] = ("dve", zadd)
                wo_mm_done[parw] = p.c("pe")
                zpar = ei % 2
                zbw = [("dve", zadd), ("pe", s2["zbs_guard"][zpar])]
                p.emit("act", lambda e, ei=ei, zpar=zpar: e.copy(
                    out=ZBS[zpar], in_=zslot(ei)), waits=zbw, inc="act")
                sqv = p.emit("act", lambda e, ei=ei, zpar=zpar: e.square(
                    out=ZBS[2 + zpar], in_=zslot(ei)), inc="act")
                if pend_stats is not None:
                    pend_stats()
                pend_stats = make_stats(ei, sqv)
                if ZDBG is not None:
                    p.emit("pool", lambda e, ei=ei, hoff=hoff: e.dma_start(
                        out=ZDBG[ei * 128:(ei + 1) * 128, hoff:hoff + HALF],
                        in_=zslot(ei)), waits=[("act", sqv)], inc="sw", amt=16)
            stats_stop = pend_stats()
            s2["wo_phase_end"] = p.c("pe")
            lnd = emit_ln_chain(stats_stop, LNW_A, LNW_B)
            for ei in range(ET):
                p.emit("dve", lambda e, ei=ei: e.tensor_tensor(
                    out=zslot(ei), in0=zslot(ei), in1=LNW_A[:, :], op=ALU.mult),
                    waits=[("act", lnd)])
                av = p.emit("dve", lambda e, ei=ei: e.tensor_tensor(
                    out=zslot(ei), in0=zslot(ei), in1=LNW_B[:, :], op=ALU.add),
                    inc="dve")
                s2["ln1_apply_last"] = av
                sv = p.emit("sync", lambda e, ei=ei, hoff=hoff: e.dma_start(
                    out=LNSCR[ei * 128:(ei + 1) * 128, hoff:hoff + HALF],
                    in_=zslot(ei)), waits=[("dve", av)], inc="st", amt=16)
                s2["ln1_store"][ei] = sv
                cv = p.emit("act", lambda e, ei=ei: e.copy(
                    out=ZB[:, ei * 2048:ei * 2048 + 1024], in_=zslot(ei)),
                    waits=[("st", sv)], inc="act")
                s2["z2b_cast"][ei] = cv
            if ZBDBG is not None and half == 0:
                for et_ in range(ET):
                    p.emit("pool", lambda e, et_=et_: e.dma_start(
                        out=ZBDBG[:, et_ * 1024:(et_ + 1) * 1024],
                        in_=ZB[:, et_ * 2048:et_ * 2048 + 1024]),
                        waits=[("act", s2["z2b_cast"][ET - 1])], inc="sw", amt=16)
            WFCF2 = [f32(o_TL, 1024), f32(o_TL + 1024, 1024)]
            fcq_free = [None, None]
            fc_mm_done = [None, None]
            tern_done = [None, None]
            fcs = {"h1_last": None, "h1_sq": [None, None], "h1_qg": [None] * 4}
            pend_rel = []

            def make_rel(hh, sc, b, v):
                def go():
                    rw = [("pe", v), ("act", fcs["h1_sq"][sc])]
                    rv = p.emit("dve", lambda e, sc=sc, b=b: e.tensor_scalar_max(
                        out=HREL[sc][:, :], in0=PS[b][:, 0:512], scalar1=0.0),
                        waits=rw, inc="dve")
                    bank_free[b] = ("dve", rv)
                    hv = p.emit("act", lambda e, hh=hh, sc=sc: e.square(
                        out=H1[:, hh * 1024 + sc * 512:hh * 1024 + sc * 512 + 512],
                        in_=HREL[sc][:, :]), waits=[("dve", rv)], inc="act")
                    fcs["h1_last"] = hv
                    fcs["h1_sq"][sc] = hv
                    if hh % 10 == 9 and sc == 1:
                        fcs["h1_qg"][hh // 10] = hv
                return go

            for hh in range(HT):
                parq = hh % 2
                stg = WFCF if parq == 0 else WFCF2
                lw = [("dve", tern_done[parq])] if tern_done[parq] else []
                if hh == 0:
                    lw += [("dve", s2["wpf_guard"][0]), ("dve", s2["wpf_guard"][1])]
                if hh <= 1:
                    lw += [("dve", s2["ln1in_guard"][0]),
                           ("dve", s2["ln1in_guard"][1])]
                lv1 = p.emit("sync", lambda e, hh=hh, stg=stg: e.dma_start(
                    out=stg[0][:, :], in_=WFCCB[hh, 0]), waits=lw, inc="hw", amt=16)
                lv2 = p.emit("pool", lambda e, hh=hh, stg=stg: e.dma_start(
                    out=stg[1][:, :], in_=WFCCB[hh, 1]), waits=lw, inc="sw", amt=16)
                qv = None
                for hb in range(2):
                    tw = [("hw", lv1), ("sw", lv2)]
                    if hh == 0 and hb == 0:
                        tw.append(("pe", s2["t0_guard"]))
                    if hb == 0:
                        tw.append(("pe", fc_mm_done[parq]))
                    p.emit("dve", lambda e, hb=hb, stg=stg: e.tensor_single_scalar(
                        out=LTMP_FC[:, hb * 1024:(hb + 1) * 1024], in_=stg[hb][:, :],
                        scalar=thr_fc[:, 1:2], op=ALU.is_lt), waits=tw)
                    qv = p.emit("dve", lambda e, hb=hb, parq=parq, stg=stg:
                                e.scalar_tensor_tensor(
                        out=WFCQ[parq][:, hb * 1024:(hb + 1) * 1024], in0=stg[hb][:, :],
                        scalar=thr_fc[:, 0:1], in1=LTMP_FC[:, hb * 1024:(hb + 1) * 1024],
                        op0=ALU.is_gt, op1=ALU.subtract), inc="dve")
                fcq_free[parq] = qv
                tern_done[parq] = qv
                if WQDBG is not None and half == 0 and hh == 0:
                    p.emit("pool", lambda e: e.dma_start(
                        out=WQDBG[:, :], in_=WFCQ[0][:, :]),
                        waits=[("dve", qv)], inc="sw", amt=16)
                for f in pend_rel:
                    f()
                pend_rel = []
                for sc in range(2):
                    b = sc if parq == 0 else 6 + sc
                    if hh < 2:
                        v = None
                        for et in range(ET):
                            w = [("act", s2["z2b_cast"][et])]
                            if et == 0:
                                w += [("dve", qv)] + bank_wait(b)
                            v = p.emit("pe", lambda e, parq=parq, sc=sc, b=b,
                                       et=et: e.matmul(
                                out=PS[b][:, 0:512],
                                lhsT=WFCQ[parq][:, et * 128:(et + 1) * 128],
                                rhs=ZB[:, et * 2048 + sc * 512:et * 2048 + sc * 512 + 512],
                                start=(et == 0), stop=(et == ET - 1)),
                                waits=w, inc="pe")
                    else:
                        w = ([("dve", qv), ("act", s2["z2b_cast"][ET - 1])]
                             + bank_wait(b))
                        def mmf(e, parq=parq, sc=sc, b=b):
                            r = None
                            for et in range(ET):
                                r = e.matmul(
                                    out=PS[b][:, 0:512],
                                    lhsT=WFCQ[parq][:, et * 128:(et + 1) * 128],
                                    rhs=ZB[:, et * 2048 + sc * 512:et * 2048 + sc * 512 + 512],
                                    start=(et == 0), stop=(et == ET - 1))
                            return r
                        v = p.emit("pe", mmf, waits=w, inc="pe")
                    fc_mm_done[parq] = v
                    pend_rel.append(make_rel(hh, sc, b, v))
            for f in pend_rel:
                f()
            pend_rel = []
            h1_last = fcs["h1_last"]
            h1_sq = fcs["h1_sq"]
            s2["t0_guard"] = p.c("pe")
            if HDBG is not None and half == 0:
                p.emit("pool", lambda e: e.dma_start(out=HDBG[:, :], in_=H1[:, 0:4096]),
                       waits=[("act", h1_last)], inc="sw", amt=16)
            stats_stop2 = None
            pend_stats = None
            pend_ptail = None
            wpf_guard = s2["wpf_guard"]
            pq_done = s2["pq_done"]
            for ei in range(ET):
                tern_q = [None] * 4
                for qb in range(4):
                    w = []
                    if ei == 0 and qb <= 1:
                        w.append(("dve", fcq_free[0]))
                        w.append(("dve", fcq_free[1]))
                    if wpf_guard[qb % 2] is not None:
                        w.append(("dve", wpf_guard[qb % 2]))
                    qu = "sync" if qb % 2 == 0 else "pool"
                    qs = "hw" if qb % 2 == 0 else "sw"
                    lv = p.emit(qu, lambda e, ei=ei, qb=qb: e.dma_start(
                        out=WPF[qb % 2][:, :], in_=WPJCB[ei, qb]),
                        waits=w, inc=qs, amt=16)
                    qv = None
                    for ch in range(2):
                        tw = [(qs, lv), ("pe", pq_done[qb])]
                        if ei == 0 and qb == 0 and ch == 0:
                            tw += [("pe", s2["t0_guard"]), ("act", h1_last)]
                        p.emit("dve", lambda e, qb=qb, ch=ch: e.tensor_single_scalar(
                            out=LTMP_PJ[:, ch * 1024:ch * 1024 + 640],
                            in_=WPF[qb % 2][:, ch * 640:(ch + 1) * 640],
                            scalar=thr_wp[:, 1:2], op=ALU.is_lt), waits=tw)
                        qv = p.emit("dve", lambda e, qb=qb, ch=ch:
                                    e.scalar_tensor_tensor(
                            out=WPQ[:, qb * 1280 + ch * 640:qb * 1280 + (ch + 1) * 640],
                            in0=WPF[qb % 2][:, ch * 640:(ch + 1) * 640],
                            scalar=thr_wp[:, 0:1],
                            in1=LTMP_PJ[:, ch * 1024:ch * 1024 + 640],
                            op0=ALU.is_gt, op1=ALU.subtract), inc="dve")
                    wpf_guard[qb % 2] = qv
                    tern_q[qb] = qv
                li = ei % 2
                lnw = [("st", s2["ln1_store"][ei]), ("dve", s2["ln1in_guard"][li])]
                lnv = p.emit("sync", lambda e, ei=ei, hoff=hoff, li=li: e.dma_start(
                    out=LN1IN[li][:, :],
                    in_=LNSCR[ei * 128:(ei + 1) * 128, hoff:hoff + HALF]),
                    waits=lnw, inc="hw", amt=16)
                if pend_ptail is not None:
                    pend_ptail()
                    pend_ptail = None
                mmv = []
                for sc in range(2):
                    b = (sc if ei % 2 == 0 else 6 + sc)
                    v = None
                    for qg in range(4):
                        w = [("dve", tern_q[qg]), ("act", fcs["h1_qg"][qg]),
                             ("hw", lnv)]
                        if qg == 0:
                            w += bank_wait(b)
                        def mmp(e, sc=sc, b=b, qg=qg):
                            r = None
                            for hh in range(qg * 10, qg * 10 + 10):
                                r = e.matmul(
                                    out=PS[b][:, 0:512],
                                    lhsT=WPQ[:, hh * 128:(hh + 1) * 128],
                                    rhs=H1[:, hh * 1024 + sc * 512:hh * 1024 + sc * 512 + 512],
                                    start=(hh == 0), stop=(hh == HT - 1))
                            return r
                        v = p.emit("pe", mmp, waits=w, inc="pe")
                        if sc == 1:
                            pq_done[qg] = v
                    mmv.append((sc, b, v))

                def make_ptail(ei=ei, li=li, mmv=mmv):
                    def go():
                        nonlocal pend_stats
                        dv = None
                        for (sc, b, v) in mmv:
                            dv = p.emit("dve", lambda e, ei=ei, sc=sc, b=b, li=li:
                                        e.scalar_tensor_tensor(
                                out=zslot(ei)[:, sc * 512:(sc + 1) * 512],
                                in0=PS[b][:, 0:512],
                                scalar=c_vec[:, 0:1],
                                in1=LN1IN[li][:, sc * 512:(sc + 1) * 512],
                                op0=ALU.mult, op1=ALU.add),
                                waits=[("pe", v)], inc="dve")
                            bank_free[b] = ("dve", dv)
                        s2["ln1in_guard"][li] = dv
                        zpar = ei % 2
                        zbw = [("dve", dv), ("pe", s2["zbs_guard"][zpar])]
                        p.emit("act", lambda e, ei=ei, zpar=zpar: e.copy(
                            out=ZBS[zpar], in_=zslot(ei)), waits=zbw, inc="act")
                        sqv = p.emit("act", lambda e, ei=ei, zpar=zpar: e.square(
                            out=ZBS[2 + zpar], in_=zslot(ei)), inc="act")
                        if pend_stats is not None:
                            pend_stats()
                        pend_stats = make_stats(ei, sqv)
                        if Z3DBG is not None:
                            p.emit("pool", lambda e, ei=ei, hoff=hoff: e.dma_start(
                                out=Z3DBG[ei * 128:(ei + 1) * 128, hoff:hoff + HALF],
                                in_=zslot(ei)), waits=[("act", sqv)], inc="sw", amt=16)
                    return go
                pend_ptail = make_ptail()
            pend_ptail()
            pend_ptail = None
            stats_stop2 = pend_stats()
            s2["proj_end"] = p.c("pe")
            lnd2 = emit_ln_chain(stats_stop2, LNW_A, LNW_B)
            for ei in range(ET):
                p.emit("dve", lambda e, ei=ei: e.tensor_tensor(
                    out=zslot(ei), in0=zslot(ei), in1=LNW_A[:, :], op=ALU.mult),
                    waits=[("act", lnd2)])
                av = p.emit("dve", lambda e, ei=ei: e.tensor_tensor(
                    out=zslot(ei), in0=zslot(ei), in1=LNW_B[:, :], op=ALU.add),
                    inc="dve")
                s2["ln2_apply_last"] = av
                sv = p.emit("sync", lambda e, ei=ei, hoff=hoff: e.dma_start(
                    out=YT[ei * 128:(ei + 1) * 128, hoff:hoff + HALF], in_=zslot(ei)),
                    waits=[("dve", av)], inc="st", amt=16)
                s2["z_free"][ei] = sv
        if MDBG is not None:
            misc_sb = ctx.enter_context(nc.sbuf_tensor("misc_sb", [128, 16], F32))
            mv1 = p.emit("dve", lambda e: e.tensor_copy(out=misc_sb[:, 0:2], in_=thr_fc[:, :]),
                         waits=[("dve", p.c("dve")), ("act", p.c("act"))], inc="dve")
            mv1 = p.emit("dve", lambda e: e.tensor_copy(out=misc_sb[:, 2:4], in_=thr_wp[:, :]),
                         inc="dve")
            mv1 = p.emit("dve", lambda e: e.tensor_copy(out=misc_sb[:, 4:5], in_=c_vec[:, :]),
                         inc="dve")
            mv1 = p.emit("dve", lambda e: e.tensor_copy(out=misc_sb[0:1, 8:16], in_=rows[0:1, :]),
                         inc="dve")
            p.emit("sync", lambda e: e.dma_start(out=MDBG[:], in_=misc_sb[:]),
                   waits=[("dve", mv1)], inc="st", amt=16)
            for ii in range(4):
                p.emit("sync", lambda e, ii=ii: e.dma_start(
                    out=LDBG[ii * 512:(ii + 1) * 512, :],
                    in_=LNSCR[ii * 512:(ii + 1) * 512, :]), inc="st", amt=16)
        p.emit("sync", lambda e: e.nop(), waits=[("st", p.c("st"))])
    else:
        p.emit("sync", lambda e: e.nop(), waits=[("st", p.c("st"))])

    semnames = ["pe", "act", "dve", "hw", "st", "sw", "pl"]
    sems = {n: ctx.enter_context(nc.semaphore(n)) for n in semnames}

    def runner(ops):
        def go(eng):
            seen = {}
            for ws, fn, inc, amt in ops:
                for (s_, v_) in ws:
                    if seen.get(s_, 0) < v_:
                        eng.wait_ge(sems[s_], v_)
                        seen[s_] = v_
                inst = fn(eng)
                if inc is not None:
                    inst.then_inc(sems[inc], amt)
        return go

    with nc.Block() as blk:
        blk.tensor(runner(p.ops["pe"]))
        blk.scalar(runner(p.ops["act"]))
        blk.vector(runner(p.ops["dve"]))
        blk.sync(runner(p.ops["sync"]))
        blk.gpsimd(runner(p.ops["pool"]))

    ctx.close()
    return nc



_PERM = np.concatenate([np.arange(0, 128, 2), np.arange(1, 128, 2)])


def prep_weights(Wqkv, Wo, w_fc, w_proj):
    wq = np.empty((NH, E, 3 * D), np.float32)
    for h in range(NH):
        wq[h, :, 0:128] = Wqkv[:, h * 128:(h + 1) * 128][:, _PERM]
        wq[h, :, 128:256] = Wqkv[:, E + h * 128:E + (h + 1) * 128][:, _PERM]
        wq[h, :, 256:384] = Wqkv[:, 2 * E + h * 128:2 * E + (h + 1) * 128]
    wqp = np.ascontiguousarray(
        wq.reshape(NH, ET, 128, 384).transpose(0, 2, 1, 3)).reshape(NH, 128, ET * 384)
    wop = np.ascontiguousarray(
        Wo.reshape(ET, 128, ET, 128).transpose(2, 1, 0, 3)).reshape(ET, 128, ET * 128)
    wfcp = np.ascontiguousarray(
        w_fc.reshape(HT, 128, 2, 8, 128).transpose(0, 2, 4, 3, 1)).reshape(
            HT, 2, 128, 1024)
    wpjp = np.ascontiguousarray(
        w_proj.reshape(ET, 128, 4, 10, 128).transpose(0, 2, 4, 3, 1)).reshape(
            ET, 4, 128, 1280)
    return wqp, wop, wfcp, wpjp


def rope_tables():
    i = np.arange(64, dtype=np.float64)
    inv = 10000.0 ** (-2.0 * i / 128.0)
    pos = np.arange(S, dtype=np.float64)
    th = np.outer(inv, pos)
    c = np.cos(th).astype(np.float32)
    s = np.sin(th).astype(np.float32)
    cc = np.ascontiguousarray(np.concatenate([c, c], 0))
    ss = np.ascontiguousarray(np.concatenate([-s, s], 0))
    return cc, ss


def causal_mask():
    k = np.arange(128)[:, None]
    q = np.arange(128)[None, :]
    return np.where(k <= q, 1.0, 0.0).astype(np.float32)


def make_in_maps(x, Wqkv, Wo, w_fc, w_proj):
    wq, wocb, wfccb, wpjcb = prep_weights(Wqkv, Wo, w_fc, w_proj)
    cc, ss = rope_tables()
    msk = causal_mask()
    ident = np.eye(128, dtype=np.float32)
    in_maps = []
    for b in range(8):
        xt = np.ascontiguousarray(x[b].T)
        in_maps.append({
            "xt_in": xt, "wqkv_h": wq, "wo_cb": wocb, "wfc_cb": wfccb,
            "wpj_cb": wpjcb, "rope_cc": cc, "rope_ss": ss,
            "mask_in": msk, "ident_in": ident,
        })
    return in_maps


_NC_CACHE = {}


def kernel(L0_attn_x, L0_attn_Wqkv, L0_attn_Wo, L0_ffn_w_fc, L0_ffn_w_proj):
    x = np.asarray(L0_attn_x, np.float32)
    in_maps = make_in_maps(
        x, np.asarray(L0_attn_Wqkv, np.float32), np.asarray(L0_attn_Wo, np.float32),
        np.asarray(L0_ffn_w_fc, np.float32), np.asarray(L0_ffn_w_proj, np.float32))
    if "full" not in _NC_CACHE:
        _NC_CACHE["full"] = build_nc()
    nc = _NC_CACHE["full"]
    res = run_bass_kernel_spmd(nc, in_maps, core_ids=list(range(8)))
    return np.stack([np.asarray(r["yt_out"]).T for r in res.results], 0)


# version 19
# speedup vs baseline: 1.6935x; 1.0708x over previous
import math
import contextlib
import numpy as np
import ml_dtypes
import concourse.bass as bass
import concourse.mybir as mybir
from concourse.bass_utils import run_bass_kernel_spmd

F32 = mybir.dt.float32
BF16 = mybir.dt.bfloat16
AF = mybir.ActivationFunctionType
ALU = mybir.AluOpType
AX = mybir.AxisListType

S = 2048
E = 2048
NH = 16
D = 128
HID = 5120
ET = E // 128
HT = HID // 128
NEG_BIAS = -8.0
MASKNEG = -1.0e9
INV_SQRT_D = 1.0 / math.sqrt(D)
N_FC = float(HID * E)
N_WP = float(E * HID)
HALF = 1024

ARENA_F32_COLS = 47616


class Prog:
    ENGS = ("pe", "act", "dve", "sync", "pool")
    DMA_LANES = {"sw": 4, "hw": 4, "st": 4}

    def __init__(self):
        self.ops = {e: [] for e in self.ENGS}
        self.cnt = {}
        self.lane_idx = {}
        self.dma_snap = {}

    def emit(self, eng, fn, waits=(), inc=None, amt=1):
        wd = {}
        for (s, v) in waits:
            if v is not None and v > 0:
                wd[s] = max(wd.get(s, 0), int(v))
        ws = list(wd.items())
        real_inc = inc
        if inc in self.DMA_LANES:
            k = self.lane_idx.get(inc, 0)
            self.lane_idx[inc] = (k + 1) % self.DMA_LANES[inc]
            real_inc = f"{inc}{k}"
        self.ops[eng].append((ws, fn, real_inc, amt))
        if inc is not None:
            self.cnt[inc] = self.cnt.get(inc, 0) + amt
            if real_inc != inc:
                self.cnt[real_inc] = self.cnt.get(real_inc, 0) + amt
                self.dma_snap[(inc, self.cnt[inc])] = {
                    f"{inc}{j}": self.cnt.get(f"{inc}{j}", 0)
                    for j in range(self.DMA_LANES[inc])
                    if self.cnt.get(f"{inc}{j}", 0) > 0}
            return self.cnt[inc]
        return None

    def c(self, sem):
        return self.cnt.get(sem, 0)


def build_nc(n_heads=NH, stage2=True, dbg=None):
    nc = bass.Bass("TRN2", target_bir_lowering=False)

    XT = nc.dram_tensor("xt_in", [E, S], F32, kind="ExternalInput").ap()
    WQKV = nc.dram_tensor("wqkv_h", [NH, 128, ET * 384], F32, kind="ExternalInput").ap()
    WOCB = nc.dram_tensor("wo_cb", [ET, 128, ET * 128], F32, kind="ExternalInput").ap()
    WFCCB = nc.dram_tensor("wfc_cb", [HT, 2, 128, 1024], F32, kind="ExternalInput").ap()
    WPJCB = nc.dram_tensor("wpj_cb", [ET, 4, 128, 1280], F32, kind="ExternalInput").ap()
    TCC = nc.dram_tensor("rope_cc", [128, S], F32, kind="ExternalInput").ap()
    TSS = nc.dram_tensor("rope_ss", [128, S], F32, kind="ExternalInput").ap()
    MASKIN = nc.dram_tensor("mask_in", [128, 128], F32, kind="ExternalInput").ap()
    IDIN = nc.dram_tensor("ident_in", [128, 128], F32, kind="ExternalInput").ap()
    YT = nc.dram_tensor("yt_out", [E, S], F32, kind="ExternalOutput").ap()
    OSCR = nc.dram_tensor("o_scr", [NH, 128, S], BF16, kind="Internal").ap()
    LNSCR = nc.dram_tensor("ln1_scr", [E, S], F32, kind="Internal").ap()
    ODBG = None
    if dbg == "oscr":
        ODBG = nc.dram_tensor("o_dbg", [NH, 128, S], F32, kind="ExternalOutput").ap()
    ZDBG = MDBG = LDBG = O2DBG = Z3DBG = HDBG = ZBDBG = WQDBG = None
    if dbg == "s2":
        ZDBG = nc.dram_tensor("z_dbg", [E, S], F32, kind="ExternalOutput").ap()
        MDBG = nc.dram_tensor("m_dbg", [128, 16], F32, kind="ExternalOutput").ap()
        LDBG = nc.dram_tensor("l_dbg", [E, S], F32, kind="ExternalOutput").ap()
        O2DBG = nc.dram_tensor("o2_dbg", [128, 16384], F32, kind="ExternalOutput").ap()
        Z3DBG = nc.dram_tensor("z3_dbg", [E, S], F32, kind="ExternalOutput").ap()
        HDBG = nc.dram_tensor("h_dbg", [128, 4096], F32, kind="ExternalOutput").ap()
        ZBDBG = nc.dram_tensor("zb_dbg", [128, 16384], F32, kind="ExternalOutput").ap()
        WQDBG = nc.dram_tensor("wq_dbg", [128, 2048], F32, kind="ExternalOutput").ap()

    p = Prog()
    ctx = contextlib.ExitStack()

    RAW = ctx.enter_context(nc.sbuf_tensor("arena", [128, ARENA_F32_COLS], F32))
    RAWB = RAW[:].bitcast(BF16)

    def f32(off, cols):
        return RAW[:, off:off + cols]

    def bf(off, cols):
        return RAWB[:, 2 * off:2 * off + cols]

    cc_sb = ctx.enter_context(nc.sbuf_tensor("cc_sb", [128, S], BF16))
    ss_sb = ctx.enter_context(nc.sbuf_tensor("ss_sb", [128, S], BF16))
    mask_sb = ctx.enter_context(nc.sbuf_tensor("mask_sb", [128, 128], BF16))
    id_sb = ctx.enter_context(nc.sbuf_tensor("id_sb", [128, 128], BF16))
    idz_sb = ctx.enter_context(nc.sbuf_tensor("idz_sb", [128, 512], BF16))
    ones_bcol = ctx.enter_context(nc.sbuf_tensor("ones_bcol", [128, 1], BF16))
    ones_fcol = ctx.enter_context(nc.sbuf_tensor("ones_fcol", [128, 1], F32))
    ones_frow = ctx.enter_context(nc.sbuf_tensor("ones_frow", [1, 128], F32))
    acc_fc = ctx.enter_context(nc.sbuf_tensor("acc_fc", [128, 80], F32))
    acc_wp = ctx.enter_context(nc.sbuf_tensor("acc_wp", [128, 80], F32))
    accT = ctx.enter_context(nc.sbuf_tensor("accT", [128, 2], F32))
    rows = ctx.enter_context(nc.sbuf_tensor("rows_sm", [1, 8], F32))
    thr_fc = ctx.enter_context(nc.sbuf_tensor("thr_fc", [128, 2], F32))
    thr_wp = ctx.enter_context(nc.sbuf_tensor("thr_wp", [128, 2], F32))
    c_vec = ctx.enter_context(nc.sbuf_tensor("c_vec", [128, 1], F32))
    rl_sb = ctx.enter_context(nc.sbuf_tensor("rl_sb", [128, 4], F32))
    consts = ctx.enter_context(nc.sbuf_tensor("consts", [128, 4], F32))
    lnrows = ctx.enter_context(nc.sbuf_tensor("lnrows", [1, 2560], F32))
    LR = {"m": lnrows[0:1, 0:512], "v": lnrows[0:1, 512:1024],
          "s": lnrows[0:1, 1024:1536], "A": lnrows[0:1, 1536:2048],
          "B": lnrows[0:1, 2048:2560]}

    PS = [ctx.enter_context(nc.psum_tensor(f"ps{i}", [128, 512], F32)) for i in range(8)]
    PSB = [PS[i][:].bitcast(BF16) for i in range(8)]

    o_Xb = 0
    o_WH0, o_WH1 = 16384, 19456
    o_Qd0, o_Qd1 = 22528, 23552
    o_Kd0, o_Kd1 = 24576, 25600
    o_Vd = 26624
    o_Vt0, o_Vt1 = 27648, 28736
    o_PT0, o_PT1 = 29824, 33920
    o_Ot = 38016
    o_Ofm = 38528
    o_Ff = 39552
    o_Fs = 41600
    o_tT = 43648
    o_tU = 44160
    o_MB = 44672
    assert 47232 <= ARENA_F32_COLS
    Xb = bf(o_Xb, 32768)
    WH = [bf(o_WH0, 6144), bf(o_WH1, 6144)]
    Qd = [bf(o_Qd0, 2048), bf(o_Qd1, 2048)]
    Kd = [bf(o_Kd0, 2048), bf(o_Kd1, 2048)]
    Vd = bf(o_Vd, 2048)
    Vt = [bf(o_Vt0, 2176), bf(o_Vt1, 2176)]
    PT = [bf(o_PT0, 8192), bf(o_PT1, 8192)]
    Ot = bf(o_Ot, 1024)
    Ofm = bf(o_Ofm, 2048)
    Ff = f32(o_Ff, 2048)
    Fs = f32(o_Fs, 2048)
    tT = f32(o_tT, 512)
    tU = f32(o_tU, 512)
    MBUF = [bf(o_MB + k * 640, 1280) for k in range(4)]

    o_Z = 0
    o_H1 = 16384
    o_T0 = 36864
    o_T1 = 39936
    o_TL = 42496
    o_T3 = 44544
    o_T4 = 45568
    assert 45568 + 2048 <= ARENA_F32_COLS
    Z = RAW[:, o_Z:o_Z + 16384]
    ZB = RAWB[:, 2 * o_Z:2 * o_Z + 32768]
    H1 = bf(o_H1, 40960)
    O2 = bf(o_H1, 16384)
    WoCB = [bf(o_H1 + 8192, 2048), bf(o_H1 + 9216, 2048)]
    LNW_A = f32(o_H1 + 10240, 1024)
    LNW_B = f32(o_H1 + 11264, 1024)
    LN1IN = [f32(o_TL, 1024), f32(o_TL + 1024, 1024)]
    WPQ = bf(o_T0, 5120)
    WFCQ = [bf(o_T0, 2048), bf(o_T0 + 1024, 2048)]
    LTMP_FC = bf(o_T0 + 2048, 2048)
    WPF = [f32(o_T1, 1280), f32(o_T1 + 1280, 1280)]
    WFCF = [f32(o_T1, 1024), f32(o_T1 + 1024, 1024)]
    HREL = [f32(o_T3, 512), f32(o_T3 + 512, 512)]
    LTMP_PJ = bf(o_T3, 2048)
    ZBS = [bf(o_T4, 1024), bf(o_T4 + 512, 1024), bf(o_T4 + 1024, 1024),
           bf(o_T4 + 1536, 1024)]

    def zslot(i):
        return Z[:, i * 1024:(i + 1) * 1024]

    bank_free = {}

    def bank_wait(i):
        w = bank_free.get(i)
        return [w] if w else []

    p.emit("pool", lambda e: e.dma_start(out=cc_sb[:], in_=TCC[:]), inc="sw", amt=16)
    p.emit("pool", lambda e: e.dma_start(out=ss_sb[:], in_=TSS[:]), inc="sw", amt=16)
    v_tabs = p.emit("pool", lambda e: e.dma_start(out=id_sb[:], in_=IDIN[:]),
                    inc="sw", amt=16)
    p.emit("pool", lambda e: e.dma_start(out=idz_sb[:, 0:128], in_=IDIN[:]),
           inc="sw", amt=16)
    v_mask = p.emit("pool", lambda e: e.dma_start(out=mask_sb[:], in_=MASKIN[:]),
                    inc="sw", amt=16)
    v_idz = p.emit("dve", lambda e: e.memset(idz_sb[:, 128:512], 0.0), inc="dve")

    XTr = XT.rearrange("(et pp) s -> pp et s", pp=128)
    v_xb = None
    for c4 in range(4):
        v_xb = p.emit("pool", lambda e, c4=c4: e.dma_start(
            out=Xb[:].rearrange("pp (et s) -> pp et s", s=2048)[:, c4 * 4:(c4 + 1) * 4, :],
            in_=XTr[:, c4 * 4:(c4 + 1) * 4, :]), inc="sw", amt=16)

    def load_wh(h, waits):
        par = h % 2
        return p.emit("pool", lambda e, h=h, par=par: e.dma_start(
            out=WH[par][:, :], in_=WQKV[h]), waits=waits, inc="sw", amt=16)

    v_wh = [None] * (NH + 1)
    v_wh[0] = load_wh(0, [])

    p.emit("dve", lambda e: e.memset(consts[:, 0:1], NEG_BIAS))
    p.emit("dve", lambda e: e.memset(consts[:, 1:2], INV_SQRT_D))
    p.emit("dve", lambda e: e.memset(consts[:, 2:3], -0.5))
    v_consts = p.emit("dve", lambda e: e.memset(consts[:, 3:4], 1e-5), inc="dve")
    p.emit("dve", lambda e: e.memset(ones_bcol[:, :], 1.0))
    p.emit("dve", lambda e: e.memset(ones_fcol[:, :], 1.0))
    p.emit("dve", lambda e: e.memset(ones_frow[:, :], 1.0))
    p.emit("dve", lambda e: e.memset(acc_fc[:, :], 0.0))
    p.emit("dve", lambda e: e.memset(acc_wp[:, :], 0.0))
    mzv = None
    for par in range(2):
        vt_v = Vt[par].rearrange("pp (kt c) -> pp kt c", c=136)
        mzv = p.emit("dve", lambda e, vt_v=vt_v: e.memset(vt_v[:, :, 128:136], 1.0),
                     inc="dve")

    st = {
        "qkv_last": [None] * NH,
        "rope_done": [None] * NH,
        "pv_prev": [None, None],
        "st_prev": [None, None],
        "strip_guard": [None, None],
        "ot_guard": None,
        "ofm_guard": None,
        "rl_guard": [None, None],
        "mb_guard": [None, None, None, None],
        "mean_lv": [None, None, None, None],
        "mean_pos": 0,
        "vtr_prev": None,
        "vt_ev": [[None, None], [None, None]],
        "mean_fc": None,
        "mean_wp": None,
        "did_fc_chain": False,
        "did_wp_chain": False,
        "trb": 6,
    }

    def next_trb():
        b = st["trb"]
        st["trb"] = 13 - b
        return b

    st["evac"] = {}
    st["rope_q"] = None
    A = {}

    def emit_qkv_tensor(h, t):
        par = h % 2
        toff = t * 128
        for sc in range(4):
            b = next_trb()
            waits = [("sw", v_xb), ("sw", v_wh[h])] + bank_wait(b)
            def mmq(e, par=par, toff=toff, sc=sc, b=b):
                r = None
                for et in range(ET):
                    r = e.matmul(
                        out=PS[b][:, 0:512],
                        lhsT=WH[par][:, et * 384 + toff:et * 384 + toff + 128],
                        rhs=Xb[:, et * 2048 + sc * 512:et * 2048 + (sc + 1) * 512],
                        start=(et == 0), stop=(et == ET - 1))
                return r
            v = p.emit("pe", mmq, waits=waits, inc="pe")
            if t < 2:
                w2 = [("pe", v)]
                if sc == 0 and t == 0 and h >= 1:
                    w2.append(("dve", st["rope_done"][h - 1]))
                if sc == 0 and t == 1:
                    w2.append(("dve", st["rope_q"]))
                ev = p.emit("act", lambda e, sc=sc, b=b: e.copy(
                    out=Ff[:, sc * 512:(sc + 1) * 512], in_=PS[b][:, 0:512]),
                    waits=w2, inc="act")
            else:
                w2 = [("pe", v)]
                if sc == 0:
                    w2.append(("pe", st["vtr_prev"]))
                ev = p.emit("act", lambda e, sc=sc, b=b: e.copy(
                    out=Vd[:, sc * 512:(sc + 1) * 512], in_=PS[b][:, 0:512]),
                    waits=w2, inc="act")
            st["evac"][(t, sc)] = ev
            bank_free[b] = ("act", ev)
        if t < 2:
            wsw = [("act", st["evac"][(t, 3)])]
            if t == 0 and h >= 1:
                wsw.append(("dve", st["rope_done"][h - 1]))
            else:
                wsw.append(("dve", st["rope_q"]))
            p.emit("sync", lambda e: e.dma_start(out=Fs[0:64, :], in_=Ff[64:128, :]),
                   waits=wsw, inc="hw", amt=16)
            swv = p.emit("sync", lambda e: e.dma_start(out=Fs[64:128, :], in_=Ff[0:64, :]),
                         inc="hw", amt=16)
            dstd = Qd[par] if t == 0 else Kd[par]
            rv = None
            for sc in range(4):
                w3 = []
                if sc == 0:
                    w3 = [("hw", swv), ("pe", st["st_prev"][par]), ("sw", v_tabs)]
                p.emit("dve", lambda e, sc=sc: e.tensor_tensor(
                    out=tT[:, :], in0=Ff[:, sc * 512:(sc + 1) * 512],
                    in1=cc_sb[:, sc * 512:(sc + 1) * 512], op=ALU.mult), waits=w3)
                p.emit("dve", lambda e, sc=sc: e.tensor_tensor(
                    out=tU[:, :], in0=Fs[:, sc * 512:(sc + 1) * 512],
                    in1=ss_sb[:, sc * 512:(sc + 1) * 512], op=ALU.mult))
                rv = p.emit("dve", lambda e, sc=sc, dstd=dstd: e.tensor_tensor(
                    out=dstd[:, sc * 512:(sc + 1) * 512], in0=tT[:, :], in1=tU[:, :],
                    op=ALU.add), inc="dve")
            if t == 0:
                st["rope_q"] = rv
            else:
                st["rope_done"][h] = rv

    def emit_vtrans(h):
        par = h % 2
        vlast = None
        for bt in range(2):
            b = next_trb()
            waits = [("act", st["evac"][(2, 2 * bt + 1)]), ("sw", v_tabs)] + bank_wait(b)
            def vtr(e, bt=bt, b=b):
                r = None
                for j in range(8):
                    si = bt * 8 + j
                    r = e.transpose(out=PSB[b][:, j * 128:(j + 1) * 128],
                                    in_=Vd[:, si * 128:(si + 1) * 128],
                                    identity=id_sb[:, :])
                return r
            v = p.emit("pe", vtr, waits=waits, inc="pe")
            vlast = v
            vt_v = Vt[par].rearrange("pp (kt c) -> pp kt c", c=136)
            w4 = [("pe", v), ("dve", mzv), ("pe", st["pv_prev"][par])]
            ev = p.emit("dve", lambda e, bt=bt, b=b, vt_v=vt_v: e.tensor_copy(
                out=vt_v[:, bt * 8:(bt + 1) * 8, 0:128], in_=PSB[b][:, 0:1024]),
                waits=w4, inc="dve")
            st["vt_ev"][par][bt] = ev
            bank_free[b] = ("dve", ev)
        st["vtr_prev"] = vlast
        st["qkv_last"][h] = p.c("pe")

    N_MCH = 2 * HT + 64

    def mean_chunk_src(ci):
        if ci < 2 * HT:
            return WFCCB[ci // 2, ci % 2], 1024
        cj = ci - 2 * HT
        return WPJCB[cj // 4, cj % 4], 1280

    def mean_emit_load(ci):
        srcv, cols = mean_chunk_src(ci)
        buf = MBUF[ci % 4]
        lw = []
        if st["mb_guard"][ci % 4] is not None:
            lw.append(st["mb_guard"][ci % 4])
        st["mean_lv"][ci % 4] = p.emit(
            "pool", lambda e, buf=buf, srcv=srcv, cols=cols: e.dma_start(
                out=buf[:, 0:cols], in_=srcv), waits=lw, inc="sw", amt=16)

    def mean_emit_red(ci):
        srcv, cols = mean_chunk_src(ci)
        buf = MBUF[ci % 4]
        if ci < 2 * HT:
            acc, col = acc_fc, ci
        else:
            acc, col = acc_wp, ci - 2 * HT
        lv = st["mean_lv"][ci % 4]
        rv = p.emit("dve", lambda e, acc=acc, col=col, cols=cols, buf=buf:
                    e.tensor_reduce(
            out=acc[:, col:col + 1], in_=buf[:, 0:cols],
            axis=AX.X, op=ALU.add, apply_absolute_value=True),
            waits=[("sw", lv)], inc="dve")
        st["mb_guard"][ci % 4] = ("dve", rv)
        if ci == 2 * HT - 1:
            st["mean_fc"] = ("dve", rv)
        if ci == N_MCH - 1:
            st["mean_wp"] = ("dve", rv)

    mean_plan = [("load", ci) for ci in range(4)]
    for ci in range(N_MCH):
        mean_plan.append(("red", ci))
        if ci + 4 < N_MCH:
            mean_plan.append(("load", ci + 4))

    def mean_tick(h, n):
        for _ in range(n):
            pos = st["mean_pos"]
            if pos >= len(mean_plan):
                return
            st["mean_pos"] += 1
            kind, ci = mean_plan[pos]
            if kind == "load":
                mean_emit_load(ci)
            else:
                mean_emit_red(ci)

    def emit_thr_chain(which):
        acc = acc_fc if which == 0 else acc_wp
        nel = N_FC if which == 0 else N_WP
        done = st["mean_fc"] if which == 0 else st["mean_wp"]
        rv = p.emit("dve", lambda e: e.tensor_reduce(
            out=accT[:, which:which + 1], in_=acc[:, 0:80], axis=AX.X, op=ALU.add),
            waits=[done], inc="dve")
        b = 7
        v = p.emit("pe", lambda e: e.matmul(
            out=PS[b][0:1, 0:1], lhsT=ones_fcol[:, :], rhs=accT[:, which:which + 1],
            start=True, stop=True), waits=[("dve", rv)] + bank_wait(b), inc="pe")
        av = p.emit("dve", lambda e: e.tensor_scalar(
            out=rows[0:1, which:which + 1], in0=PS[b][0:1, 0:1],
            scalar1=1.0 / nel, scalar2=1e-5, op0=ALU.mult, op1=ALU.max),
            waits=[("pe", v)], inc="dve")
        bank_free[b] = ("dve", av)
        tv = p.emit("dve", lambda e: e.tensor_single_scalar(
            out=rows[0:1, 2 + which:3 + which], in_=rows[0:1, which:which + 1],
            scalar=0.7, op=ALU.mult), waits=[("dve", av)], inc="dve")
        tvec = thr_fc if which == 0 else thr_wp
        v2 = p.emit("pe", lambda e: e.matmul(
            out=PS[b][0:128, 0:1], lhsT=ones_frow[:, :],
            rhs=rows[0:1, 2 + which:3 + which], start=True, stop=True),
            waits=[("dve", tv)] + bank_wait(b), inc="pe")
        c1 = p.emit("act", lambda e, tvec=tvec: e.copy(
            out=tvec[:, 0:1], in_=PS[b][0:128, 0:1]), waits=[("pe", v2)], inc="act")
        e2 = p.emit("dve", lambda e, tvec=tvec: e.tensor_single_scalar(
            out=tvec[:, 1:2], in_=PS[b][0:128, 0:1], scalar=-1.0, op=ALU.mult),
            waits=[("pe", v2), ("act", c1)], inc="dve")
        bank_free[b] = ("dve", e2)
        if which == 1:
            s1 = p.emit("dve", lambda e: e.tensor_tensor(
                out=rows[0:1, 4:5], in0=rows[0:1, 0:1], in1=rows[0:1, 0:1], op=ALU.mult),
                waits=[("dve", av)], inc="dve")
            s2 = p.emit("dve", lambda e: e.tensor_tensor(
                out=rows[0:1, 5:6], in0=rows[0:1, 4:5], in1=rows[0:1, 1:2], op=ALU.mult),
                waits=[("dve", s1)], inc="dve")
            v3 = p.emit("pe", lambda e: e.matmul(
                out=PS[b][0:128, 4:5], lhsT=ones_frow[:, :], rhs=rows[0:1, 5:6],
                start=True, stop=True), waits=[("dve", s2)] + bank_wait(b), inc="pe")
            e3 = p.emit("act", lambda e: e.copy(out=c_vec[:, 0:1], in_=PS[b][0:128, 4:5]),
                        waits=[("pe", v3)], inc="act")
            bank_free[b] = ("act", e3)

    def attn_begin(hh):
        A.clear()
        A["stb"] = 0
        A["exp"] = {}
        A["pv"] = {}
        A["stlast"] = None

    def emit_attn_group(hh, g):
        par = hh % 2
        pg = g % 2
        exp_done = A["exp"]
        pv_qt = A["pv"]
        for kt in range(4 * g + 4):
            c0 = max(0, kt * 128 - g * 512)
            b = A["stb"]
            A["stb"] = (b + 1) % 4
            w = [("dve", st["rope_done"][hh])] + bank_wait(b)
            if kt >= 4 * g:
                w += [("sw", v_mask), ("dve", v_idz)]

                def mmst(e, par=par, g=g, kt=kt, c0=c0, b=b):
                    e.matmul(
                        out=PS[b][:, c0:512],
                        lhsT=Kd[par][:, kt * 128:(kt + 1) * 128],
                        rhs=Qd[par][:, g * 512 + c0:(g + 1) * 512],
                        start=True, stop=False, skip_group_check=True)
                    return e.matmul(
                        out=PS[b][:, c0:512],
                        lhsT=mask_sb[:, :], rhs=idz_sb[:, 0:512 - c0],
                        start=False, stop=True, skip_group_check=True)
            else:
                def mmst(e, par=par, g=g, kt=kt, c0=c0, b=b):
                    return e.matmul(
                        out=PS[b][:, c0:512],
                        lhsT=Kd[par][:, kt * 128:(kt + 1) * 128],
                        rhs=Qd[par][:, g * 512 + c0:(g + 1) * 512],
                        start=True, stop=True)
            v = p.emit("pe", mmst, waits=w, inc="pe")
            A["stlast"] = v
            wexp = [("pe", v), ("pe", st["strip_guard"][pg]), ("dve", v_consts)]
            ev = p.emit("act", lambda e, pg=pg, kt=kt, c0=c0, b=b: e.activation(
                out=PT[pg][:, kt * 512 + c0:(kt + 1) * 512], in_=PS[b][:, c0:512],
                func=AF.Exp, bias=consts[:, 0:1], scale=consts[:, 1:2]),
                waits=wexp, inc="act")
            exp_done[(g, kt)] = ev
            bank_free[b] = ("act", ev)
        for qt in range(4 * g, 4 * g + 4):
            ob = 4 + qt % 2
            w = ([("act", exp_done[(g, qt)]), ("dve", st["vt_ev"][par][1])]
                 + bank_wait(ob))
            def pv(e, par=par, pg=pg, g=g, qt=qt, ob=ob):
                r = None
                for kt in range(qt + 1):
                    co = kt * 512 + qt * 128 - g * 512
                    r = e.matmul(out=PS[ob][:, 0:129],
                                 lhsT=PT[pg][:, co:co + 128],
                                 rhs=Vt[par][:, kt * 136:kt * 136 + 129],
                                 start=(kt == 0), stop=(kt == qt))
                return r
            v = p.emit("pe", pv, waits=w, inc="pe")
            pv_qt[qt] = v
            wr = [("pe", v), ("act", st["rl_guard"][qt % 2])]
            av = p.emit("act", lambda e, ob=ob, qt=qt: e.activation(
                out=rl_sb[:, 2 + qt % 2:3 + qt % 2], in_=PS[ob][:, 128:129],
                func=AF.Ln), waits=wr, inc="act")
            rv = p.emit("act", lambda e, qt=qt: e.activation(
                out=rl_sb[:, qt % 2:qt % 2 + 1],
                in_=rl_sb[:, 2 + qt % 2:3 + qt % 2],
                func=AF.Exp, scale=-1.0), inc="act")
            wo = [("act", rv)]
            if qt % 8 == 0:
                wo.append(("pe", st["ot_guard"]))
            ev = p.emit("act", lambda e, ob=ob, qt=qt: e.activation(
                out=Ot[:, (qt % 8) * 128:(qt % 8 + 1) * 128], in_=PS[ob][:, 0:128],
                func=AF.Copy, scale=rl_sb[:, qt % 2:qt % 2 + 1]),
                waits=wo, inc="act")
            st["rl_guard"][qt % 2] = ev
            bank_free[ob] = ("act", ev)
        st["strip_guard"][pg] = pv_qt[4 * g + 3]
        if g % 2 == 1:
            bb = g // 2
            b = next_trb()
            w = ([("act", st["rl_guard"][0]), ("act", st["rl_guard"][1])]
                 + bank_wait(b))
            def otr(e, b=b):
                r = None
                for j in range(8):
                    r = e.transpose(out=PSB[b][:, j * 128:(j + 1) * 128],
                                    in_=Ot[:, j * 128:(j + 1) * 128],
                                    identity=id_sb[:, :])
                return r
            v = p.emit("pe", otr, waits=w, inc="pe")
            st["ot_guard"] = v
            wf = [("pe", v)]
            if bb == 0:
                wf.append(("st", st["ofm_guard"]))
            ev = p.emit("dve", lambda e, bb=bb, b=b: e.tensor_copy(
                out=Ofm[:, bb * 1024:(bb + 1) * 1024], in_=PSB[b][:, 0:1024]),
                waits=wf, inc="dve")
            bank_free[b] = ("dve", ev)
            if bb == 1:
                st["ofm_guard"] = p.emit("sync", lambda e, hh=hh: e.dma_start(
                    out=OSCR[hh], in_=Ofm[:, :]), waits=[("dve", ev)],
                    inc="st", amt=16)
                if ODBG is not None:
                    p.emit("pool", lambda e, hh=hh: e.dma_start(
                        out=ODBG[hh], in_=Ofm[:, :]),
                        waits=[("dve", ev)], inc="st", amt=16)
        if g == 3:
            st["pv_prev"][par] = pv_qt[15]
            st["st_prev"][par] = A["stlast"]

    for h in range(n_heads + 1):
        a = h >= 1
        q = h < n_heads
        if a:
            attn_begin(h - 1)
            emit_attn_group(h - 1, 0)
        if q:
            if h + 1 < n_heads:
                wv = [("pe", st["qkv_last"][h - 1])] if h >= 1 else []
                v_wh[h + 1] = load_wh(h + 1, wv)
            emit_qkv_tensor(h, 0)
        if a:
            emit_attn_group(h - 1, 1)
        if q:
            emit_qkv_tensor(h, 1)
            if stage2:
                mean_tick(h, 5)
        if a:
            emit_attn_group(h - 1, 2)
        if q and stage2:
            mean_tick(h, 5)
        if q:
            emit_qkv_tensor(h, 2)
            emit_vtrans(h)
            if stage2:
                mean_tick(h, 4)
        if a:
            emit_attn_group(h - 1, 3)
        if q and stage2:
            mean_tick(h, 5)
            if st["mean_fc"] is not None and not st["did_fc_chain"]:
                emit_thr_chain(0)
                st["did_fc_chain"] = True
            if st["mean_wp"] is not None and not st["did_wp_chain"]:
                emit_thr_chain(1)
                st["did_wp_chain"] = True

    if stage2:
        BAR = [("pe", p.c("pe")), ("act", p.c("act")), ("dve", p.c("dve")),
               ("st", p.c("st")), ("hw", p.c("hw")), ("sw", p.c("sw")),
               ("pl", p.c("pl"))]
        for eng in ("pe", "act", "dve", "sync", "pool"):
            own = {"pe": "pe", "act": "act", "dve": "dve", "sync": None,
                   "pool": "pl"}[eng]
            w = [x for x in BAR if x[0] != own]
            p.emit(eng, lambda e: e.nop(), waits=w)

        s2 = {
            "z_free": [None] * ET,
            "z2b_cast": [None] * ET,
            "ln1_store": [None] * ET,
            "ln1_apply_last": None,
            "ln2_apply_last": None,
            "t0_guard": None,
            "t3_guard": None,
            "ln1in_guard": [None, None],
            "zbs_guard": [None, None],
            "wpf_guard": [None, None],
            "pq_done": [None] * 4,
            "wo_phase_end": None,
            "proj_end": None,
        }

        def emit_ln_chain(stats_stop, dst_A, dst_B):
            last = None
            for sc in range(2):
                sb_, qb_ = 2 + sc, 4 + sc
                p.emit("dve", lambda e, sb_=sb_: e.tensor_single_scalar(
                    out=LR["m"], in_=PS[sb_][0:1, 0:512], scalar=1.0 / E, op=ALU.mult),
                    waits=[("pe", stats_stop)])
                p.emit("dve", lambda e, qb_=qb_: e.tensor_single_scalar(
                    out=LR["v"], in_=PS[qb_][0:1, 0:512], scalar=1.0 / E, op=ALU.mult))
                p.emit("dve", lambda e: e.tensor_tensor(
                    out=LR["s"], in0=LR["m"], in1=LR["m"], op=ALU.mult))
                dv = p.emit("dve", lambda e: e.tensor_tensor(
                    out=LR["v"], in0=LR["v"], in1=LR["s"], op=ALU.subtract), inc="dve")
                av = p.emit("act", lambda e: e.activation(
                    out=LR["v"], in_=LR["v"], func=AF.Ln, bias=consts[0:1, 3:4]),
                    waits=[("dve", dv)], inc="act")
                av = p.emit("act", lambda e: e.activation(
                    out=LR["A"], in_=LR["v"], func=AF.Exp, scale=consts[0:1, 2:3]),
                    waits=[("act", av)], inc="act")
                bv = p.emit("dve", lambda e: e.scalar_tensor_tensor(
                    out=LR["B"], in0=LR["m"], scalar=-1.0, in1=LR["A"],
                    op0=ALU.mult, op1=ALU.mult), waits=[("act", av)], inc="dve")
                for (bk, src, dst) in ((6, LR["A"], dst_A), (7, LR["B"], dst_B)):
                    v1 = p.emit("pe", lambda e, bk=bk, src=src: e.matmul(
                        out=PS[bk][0:128, 0:512], lhsT=ones_frow[:, :], rhs=src,
                        start=True, stop=True), waits=[("dve", bv)] + bank_wait(bk),
                        inc="pe")
                    e1 = p.emit("act", lambda e, sc=sc, bk=bk, dst=dst: e.copy(
                        out=dst[:, sc * 512:(sc + 1) * 512], in_=PS[bk][0:128, 0:512]),
                        waits=[("pe", v1)], inc="act")
                    bank_free[bk] = ("act", e1)
                    last = e1
            for bk in (2, 3, 4, 5):
                bank_free[bk] = ("act", last)
            return last

        def make_stats(ei, sqv):
            zpar = ei % 2

            def go():
                vlast = None
                for sc in range(2):
                    v = p.emit("pe", lambda e, sc=sc, ei=ei, zpar=zpar: e.matmul(
                        out=PS[2 + sc][0:1, 0:512], lhsT=ones_bcol[:, :],
                        rhs=ZBS[zpar][:, sc * 512:(sc + 1) * 512],
                        start=(ei == 0), stop=(ei == ET - 1), skip_group_check=True),
                        waits=[("act", sqv)], inc="pe")
                    v = p.emit("pe", lambda e, sc=sc, ei=ei, zpar=zpar: e.matmul(
                        out=PS[4 + sc][0:1, 0:512], lhsT=ones_bcol[:, :],
                        rhs=ZBS[2 + zpar][:, sc * 512:(sc + 1) * 512],
                        start=(ei == 0), stop=(ei == ET - 1), skip_group_check=True),
                        inc="pe")
                    vlast = v
                s2["zbs_guard"][zpar] = vlast
                return vlast
            return go

        st_oscr = p.c("st")
        for half in range(2):
            hoff = half * HALF
            o2w = [("st", st_oscr)]
            if half == 1:
                o2w.append(("pe", s2["proj_end"]))
            o2v = p.emit("sync", lambda e, hoff=hoff: e.dma_start(
                out=O2[:].rearrange("pp (et s) -> pp et s", s=1024)[:, 0:8, :],
                in_=OSCR[0:8, :, hoff:hoff + HALF].rearrange("et pp s -> pp et s")),
                waits=o2w, inc="hw", amt=16)
            o2v_b = p.emit("pool", lambda e, hoff=hoff: e.dma_start(
                out=O2[:].rearrange("pp (et s) -> pp et s", s=1024)[:, 8:16, :],
                in_=OSCR[8:16, :, hoff:hoff + HALF].rearrange("et pp s -> pp et s")),
                waits=o2w, inc="sw", amt=16)
            if O2DBG is not None and half == 0:
                p.emit("pool", lambda e: e.dma_start(out=O2DBG[:], in_=O2[:, :]),
                       waits=[("hw", o2v)], inc="sw", amt=16)
            wo_mm_done = [None, None]
            stats_stop = None
            pend_stats = None
            for ei in range(ET):
                parw = ei % 2
                wv = [("pe", wo_mm_done[parw])] if ei >= 2 else []
                if ei < 2 and half == 1:
                    wv.append(("pe", s2["proj_end"]))
                wcv = p.emit("pool", lambda e, ei=ei, parw=parw: e.dma_start(
                    out=WoCB[parw][:, :], in_=WOCB[ei]), waits=wv, inc="sw", amt=16)
                xw = [("st", s2["z_free"][ei])] if s2["z_free"][ei] else []
                xv = p.emit("sync", lambda e, ei=ei, hoff=hoff: e.dma_start(
                    out=zslot(ei), in_=XT[ei * 128:(ei + 1) * 128, hoff:hoff + HALF]),
                    waits=xw, inc="hw", amt=16)
                zadd = None
                for sc in range(2):
                    b = sc
                    w = [("sw", wcv), ("hw", o2v), ("sw", o2v_b)] + bank_wait(b)
                    def mmw(e, parw=parw, sc=sc, b=b):
                        r = None
                        for et in range(ET):
                            r = e.matmul(
                                out=PS[b][:, 0:512],
                                lhsT=WoCB[parw][:, et * 128:(et + 1) * 128],
                                rhs=O2[:, et * 1024 + sc * 512:et * 1024 + sc * 512 + 512],
                                start=(et == 0), stop=(et == ET - 1))
                        return r
                    v = p.emit("pe", mmw, waits=w, inc="pe")
                    zadd = p.emit("dve", lambda e, ei=ei, sc=sc, b=b: e.tensor_tensor(
                        out=zslot(ei)[:, sc * 512:(sc + 1) * 512],
                        in0=zslot(ei)[:, sc * 512:(sc + 1) * 512],
                        in1=PS[b][:, 0:512], op=ALU.add),
                        waits=[("pe", v), ("hw", xv)], inc="dve")
                    bank_free[b] = ("dve", zadd)
                wo_mm_done[parw] = p.c("pe")
                zpar = ei % 2
                zbw = [("dve", zadd), ("pe", s2["zbs_guard"][zpar])]
                p.emit("act", lambda e, ei=ei, zpar=zpar: e.copy(
                    out=ZBS[zpar], in_=zslot(ei)), waits=zbw, inc="act")
                sqv = p.emit("act", lambda e, ei=ei, zpar=zpar: e.square(
                    out=ZBS[2 + zpar], in_=zslot(ei)), inc="act")
                if pend_stats is not None:
                    pend_stats()
                pend_stats = make_stats(ei, sqv)
                if ZDBG is not None:
                    p.emit("pool", lambda e, ei=ei, hoff=hoff: e.dma_start(
                        out=ZDBG[ei * 128:(ei + 1) * 128, hoff:hoff + HALF],
                        in_=zslot(ei)), waits=[("act", sqv)], inc="sw", amt=16)
            stats_stop = pend_stats()
            s2["wo_phase_end"] = p.c("pe")
            lnd = emit_ln_chain(stats_stop, LNW_A, LNW_B)
            for ei in range(ET):
                eng, sem = ("dve", "dve")
                mv = p.emit(eng, lambda e, ei=ei: e.tensor_tensor(
                    out=zslot(ei), in0=zslot(ei), in1=LNW_A[:, :], op=ALU.mult),
                    waits=[("act", lnd)], inc=sem)
                av = p.emit(eng, lambda e, ei=ei: e.tensor_tensor(
                    out=zslot(ei), in0=zslot(ei), in1=LNW_B[:, :], op=ALU.add),
                    waits=[(sem, mv)], inc=sem)
                s2["ln1_apply_last"] = av
                sv = p.emit("sync", lambda e, ei=ei, hoff=hoff: e.dma_start(
                    out=LNSCR[ei * 128:(ei + 1) * 128, hoff:hoff + HALF],
                    in_=zslot(ei)), waits=[(sem, av)], inc="st", amt=16)
                s2["ln1_store"][ei] = sv
                cv = p.emit("act", lambda e, ei=ei: e.copy(
                    out=ZB[:, ei * 2048:ei * 2048 + 1024], in_=zslot(ei)),
                    waits=[("st", sv)], inc="act")
                s2["z2b_cast"][ei] = cv
            if ZBDBG is not None and half == 0:
                for et_ in range(ET):
                    p.emit("pool", lambda e, et_=et_: e.dma_start(
                        out=ZBDBG[:, et_ * 1024:(et_ + 1) * 1024],
                        in_=ZB[:, et_ * 2048:et_ * 2048 + 1024]),
                        waits=[("act", s2["z2b_cast"][ET - 1])], inc="sw", amt=16)
            WFCF2 = [f32(o_TL, 1024), f32(o_TL + 1024, 1024)]
            fcq_free = [None, None]
            fc_mm_done = [None, None]
            tern_done = [None, None]
            fcs = {"h1_last": None, "h1_sq": [None, None], "h1_qg": [None] * 4}
            pend_rel = []

            def make_rel(hh, sc, b, v):
                def go():
                    rw = [("pe", v), ("act", fcs["h1_sq"][sc])]
                    rv = p.emit("dve", lambda e, sc=sc, b=b: e.tensor_scalar_max(
                        out=HREL[sc][:, :], in0=PS[b][:, 0:512], scalar1=0.0),
                        waits=rw, inc="dve")
                    bank_free[b] = ("dve", rv)
                    hv = p.emit("act", lambda e, hh=hh, sc=sc: e.square(
                        out=H1[:, hh * 1024 + sc * 512:hh * 1024 + sc * 512 + 512],
                        in_=HREL[sc][:, :]), waits=[("dve", rv)], inc="act")
                    fcs["h1_last"] = hv
                    fcs["h1_sq"][sc] = hv
                    if hh % 10 == 9 and sc == 1:
                        fcs["h1_qg"][hh // 10] = hv
                return go

            for hh in range(HT):
                parq = hh % 2
                stg = WFCF if parq == 0 else WFCF2
                lw = [("dve", tern_done[parq])] if tern_done[parq] else []
                if hh == 0:
                    lw += [("dve", s2["wpf_guard"][0]), ("dve", s2["wpf_guard"][1])]
                if hh <= 1:
                    lw += [("dve", s2["ln1in_guard"][0]),
                           ("dve", s2["ln1in_guard"][1])]
                lv1 = p.emit("sync", lambda e, hh=hh, stg=stg: e.dma_start(
                    out=stg[0][:, :], in_=WFCCB[hh, 0]), waits=lw, inc="hw", amt=16)
                lv2 = p.emit("pool", lambda e, hh=hh, stg=stg: e.dma_start(
                    out=stg[1][:, :], in_=WFCCB[hh, 1]), waits=lw, inc="sw", amt=16)
                qv = None
                for hb in range(2):
                    tw = [("hw", lv1), ("sw", lv2)]
                    if hh == 0 and hb == 0:
                        tw.append(("pe", s2["t0_guard"]))
                    if hb == 0:
                        tw.append(("pe", fc_mm_done[parq]))
                    lt = p.emit("dve", lambda e, hb=hb, stg=stg: e.tensor_single_scalar(
                        out=LTMP_FC[:, hb * 1024:(hb + 1) * 1024], in_=stg[hb][:, :],
                        scalar=thr_fc[:, 1:2], op=ALU.is_lt), waits=tw, inc="dve")
                    qv = p.emit("dve", lambda e, hb=hb, parq=parq, stg=stg:
                                e.scalar_tensor_tensor(
                        out=WFCQ[parq][:, hb * 1024:(hb + 1) * 1024], in0=stg[hb][:, :],
                        scalar=thr_fc[:, 0:1], in1=LTMP_FC[:, hb * 1024:(hb + 1) * 1024],
                        op0=ALU.is_gt, op1=ALU.subtract),
                        waits=[("dve", lt)], inc="dve")
                fcq_free[parq] = qv
                tern_done[parq] = qv
                if WQDBG is not None and half == 0 and hh == 0:
                    p.emit("pool", lambda e: e.dma_start(
                        out=WQDBG[:, :], in_=WFCQ[0][:, :]),
                        waits=[("dve", qv)], inc="sw", amt=16)
                for f in pend_rel:
                    f()
                pend_rel = []
                for sc in range(2):
                    b = sc if parq == 0 else 6 + sc
                    if hh < 2:
                        v = None
                        for et in range(ET):
                            w = [("act", s2["z2b_cast"][et])]
                            if et == 0:
                                w += [("dve", qv)] + bank_wait(b)
                            v = p.emit("pe", lambda e, parq=parq, sc=sc, b=b,
                                       et=et: e.matmul(
                                out=PS[b][:, 0:512],
                                lhsT=WFCQ[parq][:, et * 128:(et + 1) * 128],
                                rhs=ZB[:, et * 2048 + sc * 512:et * 2048 + sc * 512 + 512],
                                start=(et == 0), stop=(et == ET - 1)),
                                waits=w, inc="pe")
                    else:
                        w = ([("dve", qv), ("act", s2["z2b_cast"][ET - 1])]
                             + bank_wait(b))
                        def mmf(e, parq=parq, sc=sc, b=b):
                            r = None
                            for et in range(ET):
                                r = e.matmul(
                                    out=PS[b][:, 0:512],
                                    lhsT=WFCQ[parq][:, et * 128:(et + 1) * 128],
                                    rhs=ZB[:, et * 2048 + sc * 512:et * 2048 + sc * 512 + 512],
                                    start=(et == 0), stop=(et == ET - 1))
                            return r
                        v = p.emit("pe", mmf, waits=w, inc="pe")
                    fc_mm_done[parq] = v
                    pend_rel.append(make_rel(hh, sc, b, v))
            for f in pend_rel:
                f()
            pend_rel = []
            h1_last = fcs["h1_last"]
            h1_sq = fcs["h1_sq"]
            s2["t0_guard"] = p.c("pe")
            if HDBG is not None and half == 0:
                p.emit("pool", lambda e: e.dma_start(out=HDBG[:, :], in_=H1[:, 0:4096]),
                       waits=[("act", h1_last)], inc="sw", amt=16)
            stats_stop2 = None
            pend_stats = None
            pend_ptail = None
            wpf_guard = s2["wpf_guard"]
            pq_done = s2["pq_done"]
            for ei in range(ET):
                tern_q = [None] * 4
                for qb in range(4):
                    w = []
                    if ei == 0 and qb <= 1:
                        w.append(("dve", fcq_free[0]))
                        w.append(("dve", fcq_free[1]))
                    if wpf_guard[qb % 2] is not None:
                        w.append(("dve", wpf_guard[qb % 2]))
                    qu = "sync" if qb % 2 == 0 else "pool"
                    qs = "hw" if qb % 2 == 0 else "sw"
                    lv = p.emit(qu, lambda e, ei=ei, qb=qb: e.dma_start(
                        out=WPF[qb % 2][:, :], in_=WPJCB[ei, qb]),
                        waits=w, inc=qs, amt=16)
                    qv = None
                    for ch in range(2):
                        tw = [(qs, lv), ("pe", pq_done[qb])]
                        if ei == 0 and qb == 0 and ch == 0:
                            tw += [("pe", s2["t0_guard"]), ("act", h1_last)]
                        lt = p.emit("dve", lambda e, qb=qb, ch=ch: e.tensor_single_scalar(
                            out=LTMP_PJ[:, ch * 1024:ch * 1024 + 640],
                            in_=WPF[qb % 2][:, ch * 640:(ch + 1) * 640],
                            scalar=thr_wp[:, 1:2], op=ALU.is_lt), waits=tw, inc="dve")
                        qv = p.emit("dve", lambda e, qb=qb, ch=ch:
                                    e.scalar_tensor_tensor(
                            out=WPQ[:, qb * 1280 + ch * 640:qb * 1280 + (ch + 1) * 640],
                            in0=WPF[qb % 2][:, ch * 640:(ch + 1) * 640],
                            scalar=thr_wp[:, 0:1],
                            in1=LTMP_PJ[:, ch * 1024:ch * 1024 + 640],
                            op0=ALU.is_gt, op1=ALU.subtract),
                            waits=[("dve", lt)], inc="dve")
                    wpf_guard[qb % 2] = qv
                    tern_q[qb] = qv
                li = ei % 2
                lnw = [("st", s2["ln1_store"][ei]), ("dve", s2["ln1in_guard"][li])]
                lnv = p.emit("sync", lambda e, ei=ei, hoff=hoff, li=li: e.dma_start(
                    out=LN1IN[li][:, :],
                    in_=LNSCR[ei * 128:(ei + 1) * 128, hoff:hoff + HALF]),
                    waits=lnw, inc="hw", amt=16)
                if pend_ptail is not None:
                    pend_ptail()
                    pend_ptail = None
                mmv = []
                for sc in range(2):
                    b = (sc if ei % 2 == 0 else 6 + sc)
                    v = None
                    for qg in range(4):
                        w = [("dve", tern_q[qg]), ("act", fcs["h1_qg"][qg]),
                             ("hw", lnv)]
                        if qg == 0:
                            w += bank_wait(b)
                        def mmp(e, sc=sc, b=b, qg=qg):
                            r = None
                            for hh in range(qg * 10, qg * 10 + 10):
                                r = e.matmul(
                                    out=PS[b][:, 0:512],
                                    lhsT=WPQ[:, hh * 128:(hh + 1) * 128],
                                    rhs=H1[:, hh * 1024 + sc * 512:hh * 1024 + sc * 512 + 512],
                                    start=(hh == 0), stop=(hh == HT - 1))
                            return r
                        v = p.emit("pe", mmp, waits=w, inc="pe")
                        if sc == 1:
                            pq_done[qg] = v
                    mmv.append((sc, b, v))

                def make_ptail(ei=ei, li=li, mmv=mmv):
                    def go():
                        nonlocal pend_stats
                        dv = None
                        for (sc, b, v) in mmv:
                            dv = p.emit("dve", lambda e, ei=ei, sc=sc, b=b, li=li:
                                        e.scalar_tensor_tensor(
                                out=zslot(ei)[:, sc * 512:(sc + 1) * 512],
                                in0=PS[b][:, 0:512],
                                scalar=c_vec[:, 0:1],
                                in1=LN1IN[li][:, sc * 512:(sc + 1) * 512],
                                op0=ALU.mult, op1=ALU.add),
                                waits=[("pe", v)], inc="dve")
                            bank_free[b] = ("dve", dv)
                        s2["ln1in_guard"][li] = dv
                        zpar = ei % 2
                        zbw = [("dve", dv), ("pe", s2["zbs_guard"][zpar])]
                        p.emit("act", lambda e, ei=ei, zpar=zpar: e.copy(
                            out=ZBS[zpar], in_=zslot(ei)), waits=zbw, inc="act")
                        sqv = p.emit("act", lambda e, ei=ei, zpar=zpar: e.square(
                            out=ZBS[2 + zpar], in_=zslot(ei)), inc="act")
                        if pend_stats is not None:
                            pend_stats()
                        pend_stats = make_stats(ei, sqv)
                        if Z3DBG is not None:
                            p.emit("pool", lambda e, ei=ei, hoff=hoff: e.dma_start(
                                out=Z3DBG[ei * 128:(ei + 1) * 128, hoff:hoff + HALF],
                                in_=zslot(ei)), waits=[("act", sqv)], inc="sw", amt=16)
                    return go
                pend_ptail = make_ptail()
            pend_ptail()
            pend_ptail = None
            stats_stop2 = pend_stats()
            s2["proj_end"] = p.c("pe")
            lnd2 = emit_ln_chain(stats_stop2, LNW_A, LNW_B)
            for ei in range(ET):
                eng, sem = ("dve", "dve")
                mv = p.emit(eng, lambda e, ei=ei: e.tensor_tensor(
                    out=zslot(ei), in0=zslot(ei), in1=LNW_A[:, :], op=ALU.mult),
                    waits=[("act", lnd2)], inc=sem)
                av = p.emit(eng, lambda e, ei=ei: e.tensor_tensor(
                    out=zslot(ei), in0=zslot(ei), in1=LNW_B[:, :], op=ALU.add),
                    waits=[(sem, mv)], inc=sem)
                s2["ln2_apply_last"] = av
                sv = p.emit("sync", lambda e, ei=ei, hoff=hoff: e.dma_start(
                    out=YT[ei * 128:(ei + 1) * 128, hoff:hoff + HALF], in_=zslot(ei)),
                    waits=[(sem, av)], inc="st", amt=16)
                s2["z_free"][ei] = sv
        if MDBG is not None:
            misc_sb = ctx.enter_context(nc.sbuf_tensor("misc_sb", [128, 16], F32))
            mv1 = p.emit("dve", lambda e: e.tensor_copy(out=misc_sb[:, 0:2], in_=thr_fc[:, :]),
                         waits=[("dve", p.c("dve")), ("act", p.c("act"))], inc="dve")
            mv1 = p.emit("dve", lambda e: e.tensor_copy(out=misc_sb[:, 2:4], in_=thr_wp[:, :]),
                         inc="dve")
            mv1 = p.emit("dve", lambda e: e.tensor_copy(out=misc_sb[:, 4:5], in_=c_vec[:, :]),
                         inc="dve")
            mv1 = p.emit("dve", lambda e: e.tensor_copy(out=misc_sb[0:1, 8:16], in_=rows[0:1, :]),
                         inc="dve")
            p.emit("sync", lambda e: e.dma_start(out=MDBG[:], in_=misc_sb[:]),
                   waits=[("dve", mv1)], inc="st", amt=16)
            for ii in range(4):
                p.emit("sync", lambda e, ii=ii: e.dma_start(
                    out=LDBG[ii * 512:(ii + 1) * 512, :],
                    in_=LNSCR[ii * 512:(ii + 1) * 512, :]), inc="st", amt=16)
        p.emit("sync", lambda e: e.nop(), waits=[("st", p.c("st"))])
    else:
        p.emit("sync", lambda e: e.nop(), waits=[("st", p.c("st"))])

    semnames = ["pe", "act", "dve", "pl"]
    for base, nl in Prog.DMA_LANES.items():
        semnames += [f"{base}{j}" for j in range(nl)]
    sems = {n: ctx.enter_context(nc.semaphore(n)) for n in semnames}

    def runner(ops):
        def go(eng):
            seen = {}

            def wait1(s_, v_):
                if seen.get(s_, 0) < v_:
                    eng.wait_ge(sems[s_], v_)
                    seen[s_] = v_

            for ws, fn, inc, amt in ops:
                for (s_, v_) in ws:
                    if s_ in Prog.DMA_LANES:
                        for ln, lv in p.dma_snap[(s_, v_)].items():
                            wait1(ln, lv)
                    else:
                        wait1(s_, v_)
                inst = fn(eng)
                if inc is not None:
                    inst.then_inc(sems[inc], amt)
        return go

    with nc.Block() as blk:
        blk.tensor(runner(p.ops["pe"]))
        blk.scalar(runner(p.ops["act"]))
        blk.vector(runner(p.ops["dve"]))
        blk.sync(runner(p.ops["sync"]))
        blk.gpsimd(runner(p.ops["pool"]))

    ctx.close()
    return nc



_PERM = np.concatenate([np.arange(0, 128, 2), np.arange(1, 128, 2)])


def prep_weights(Wqkv, Wo, w_fc, w_proj):
    wq = np.empty((NH, E, 3 * D), np.float32)
    for h in range(NH):
        wq[h, :, 0:128] = Wqkv[:, h * 128:(h + 1) * 128][:, _PERM]
        wq[h, :, 128:256] = Wqkv[:, E + h * 128:E + (h + 1) * 128][:, _PERM]
        wq[h, :, 256:384] = Wqkv[:, 2 * E + h * 128:2 * E + (h + 1) * 128]
    wqp = np.ascontiguousarray(
        wq.reshape(NH, ET, 128, 384).transpose(0, 2, 1, 3)).reshape(NH, 128, ET * 384)
    wop = np.ascontiguousarray(
        Wo.reshape(ET, 128, ET, 128).transpose(2, 1, 0, 3)).reshape(ET, 128, ET * 128)
    wfcp = np.ascontiguousarray(
        w_fc.reshape(HT, 128, 2, 8, 128).transpose(0, 2, 4, 3, 1)).reshape(
            HT, 2, 128, 1024)
    wpjp = np.ascontiguousarray(
        w_proj.reshape(ET, 128, 4, 10, 128).transpose(0, 2, 4, 3, 1)).reshape(
            ET, 4, 128, 1280)
    return wqp, wop, wfcp, wpjp


def rope_tables():
    i = np.arange(64, dtype=np.float64)
    inv = 10000.0 ** (-2.0 * i / 128.0)
    pos = np.arange(S, dtype=np.float64)
    th = np.outer(inv, pos)
    c = np.cos(th).astype(np.float32)
    s = np.sin(th).astype(np.float32)
    cc = np.ascontiguousarray(np.concatenate([c, c], 0))
    ss = np.ascontiguousarray(np.concatenate([-s, s], 0))
    return cc, ss


def causal_mask():
    q = np.arange(128)[:, None]
    k = np.arange(128)[None, :]
    return np.where(k <= q, 0.0, MASKNEG).astype(np.float32)


def make_in_maps(x, Wqkv, Wo, w_fc, w_proj):
    wq, wocb, wfccb, wpjcb = prep_weights(Wqkv, Wo, w_fc, w_proj)
    cc, ss = rope_tables()
    msk = causal_mask()
    ident = np.eye(128, dtype=np.float32)
    in_maps = []
    for b in range(8):
        xt = np.ascontiguousarray(x[b].T)
        in_maps.append({
            "xt_in": xt, "wqkv_h": wq, "wo_cb": wocb, "wfc_cb": wfccb,
            "wpj_cb": wpjcb, "rope_cc": cc, "rope_ss": ss,
            "mask_in": msk, "ident_in": ident,
        })
    return in_maps


_NC_CACHE = {}


def kernel(L0_attn_x, L0_attn_Wqkv, L0_attn_Wo, L0_ffn_w_fc, L0_ffn_w_proj):
    x = np.asarray(L0_attn_x, np.float32)
    in_maps = make_in_maps(
        x, np.asarray(L0_attn_Wqkv, np.float32), np.asarray(L0_attn_Wo, np.float32),
        np.asarray(L0_ffn_w_fc, np.float32), np.asarray(L0_ffn_w_proj, np.float32))
    if "full" not in _NC_CACHE:
        _NC_CACHE["full"] = build_nc()
    nc = _NC_CACHE["full"]
    res = run_bass_kernel_spmd(nc, in_maps, core_ids=list(range(8)))
    return np.stack([np.asarray(r["yt_out"]).T for r in res.results], 0)
